# version 17
# baseline (speedup 1.0000x reference)
import numpy as np
import contextlib
import concourse.bass as bass
import concourse.mybir as mybir
from concourse.bass_utils import run_bass_kernel_spmd

F32 = mybir.dt.float32
BF16 = mybir.dt.bfloat16
AF = mybir.ActivationFunctionType
ALU = mybir.AluOpType
AX = mybir.AxisListType

D = 1024
S = 4096
NBLK = 16
EPS = 1e-5


SCHED_LOG = []


def build_core(n_sblk=NBLK, dbg=False):
    nc = bass.Bass("TRN2", target_bir_lowering=False)
    x_d = nc.dram_tensor("x", [S, D], F32, kind="ExternalInput").ap()
    wq_d = nc.dram_tensor("wq", [D, D], F32, kind="ExternalInput").ap()
    wk_d = nc.dram_tensor("wk", [64, D], F32, kind="ExternalInput").ap()
    wv_d = nc.dram_tensor("wv", [64, D], F32, kind="ExternalInput").ap()
    wo_d = nc.dram_tensor("wo", [D, D], F32, kind="ExternalInput").ap()
    scl_d = nc.dram_tensor("scl", [1], F32, kind="ExternalInput").ap()
    fr_d = nc.dram_tensor("fr", [D, D], F32, kind="ExternalInput").ap()
    fk_d = nc.dram_tensor("fk", [D, D], F32, kind="ExternalInput").ap()
    fv_d = nc.dram_tensor("fv", [D, D], F32, kind="ExternalInput").ap()
    mur_d = nc.dram_tensor("mur", [D], F32, kind="ExternalInput").ap()
    muk_d = nc.dram_tensor("muk", [D], F32, kind="ExternalInput").ap()
    y_d = nc.dram_tensor("y", [S, D], F32, kind="ExternalOutput").ap()

    x_t = x_d.rearrange("(a p) c -> p a c", p=128)
    y_t = y_d.rearrange("(a p) c -> p a c", p=128)

    ops = {"tensor": [], "scalar": [], "vector": [], "gpsimd": [], "sync": []}
    ctr = {"pe": 0, "act": 0, "dve": 0, "gp": 0, "x": 0, "w": 0, "out": 0, "sc": 0}
    sems = {}
    hwm = {}

    def EI(eng, fn, sem, n=1):
        ctr[sem] += n
        SCHED_LOG.append(("I", eng, sem, ctr[sem], fn.__code__.co_firstlineno))
        ops[eng].append(lambda e, fn=fn, sem=sem, n=n: fn(e).then_inc(sems[sem], n))
        return (sem, ctr[sem])

    def W(eng, tok):
        if tok is None:
            return
        sem, v = tok
        if v <= 0 or hwm.get((eng, sem), -1) >= v:
            return
        hwm[(eng, sem)] = v
        ops[eng].append(lambda e, sem=sem, v=v: e.wait_ge(sems[sem], v))

    def tmax(*toks):
        toks = [t for t in toks if t is not None and t[1] > 0]
        if not toks:
            return None
        dom = toks[0][0]
        assert all(t[0] == dom for t in toks), toks
        return (dom, max(t[1] for t in toks))

    _st = contextlib.ExitStack()
    with _st:
        def sb(name, shape, dt=BF16):
            return _st.enter_context(nc.sbuf_tensor(name, shape, dt))
        wq_t = sb("wq_t", [128, 8192])
        wo_t = sb("wo_t", [128, 8192])
        fr_t = sb("fr_t", [128, 8192])
        fk_t = sb("fk_t", [128, 8192])
        fv_t = sb("fv_t", [128, 8192])
        wkv_t = sb("wkv_t", [128, 1024])
        arena_t = sb("arena", [128, 8192])
        wnat = arena_t[:, :]
        xrT = arena_t[:, 0:2048]
        xkT = arena_t[:, 2048:4096]
        rT = arena_t[:, 4096:6144]
        kaT = arena_t[:, 6144:8192]
        kTs2 = sb("kTs2", [128, 4096])
        v_aug = sb("v_aug", [128, 2080])
        xt = sb("xt", [128, 6144], F32)
        st1 = sb("st1", [128, 64], F32)
        h1tok = sb("h1tok", [128, 1024])
        h1t_s = sb("h1t_s", [128, 2048])
        qT_s = sb("qT_s", [128, 4096])
        qsq = sb("qsq", [128, 512])
        nacc = sb("nacc", [1, 512], F32)
        P_sb = sb("P_sb", [128, 2048])
        attnT = sb("attnT", [128, 4096])
        accA = sb("accA", [128, 512])
        rrow = sb("rrow", [128, 512])
        recipb = sb("recipb", [128, 512])
        normB = sb("normB", [128, 256])
        att_o = sb("att_o", [128, 2048], F32)
        h2tok = sb("h2tok", [128, 1024])
        h2tok2 = sb("h2tok2", [128, 1024])
        h2T = sb("h2T", [128, 2056])
        diffT = sb("diffT", [128, 2048])
        ffT = sb("ffT", [128, 2048])
        katmp = sb("katmp", [128, 256])
        ksq = sb("ksq", [128, 256])
        vT_tmp = sb("vT_tmp", [128, 256])
        out_sb = sb("out_sb", [128, 2048], F32)
        stt = sb("stt", [128, 32], F32)
        sacc = sb("sacc", [128, 8], F32)
        junk = sb("junk", [128, 1024])
        dbg1 = sb("dbg1", [128, 16], F32)
        dbg2 = sb("dbg2", [128, 1024], F32)
        ident = sb("ident", [128, 128])
        ones2 = sb("ones2", [128, 128])
        onesc = sb("onesc", [128, 1])
        mur_c = sb("mur_c", [128, 8], F32)
        muk_c = sb("muk_c", [128, 8], F32)
        epsc = sb("epsc", [128, 1], F32)
        cinv = sb("cinv", [128, 1], F32)
        czero = sb("czero", [128, 1], F32)
        cmhalf = sb("cmhalf", [128, 1], F32)
        sc11 = sb("sc11", [1, 2], F32)
        smb = sb("smb", [1, 4])
        sm = sb("sm", [1, 16], F32)
        bias_col = sb("bias_col", [128, 2], F32)
        sc2col = sb("sc2col", [128, 1], F32)
        ring = _st.enter_context(nc.psum_tensor("ring", [128, 2048], F32))
        acc2 = _st.enter_context(nc.psum_tensor("acc2", [128, 1024], F32))
        util_t = _st.enter_context(nc.psum_tensor("util", [128, 2048], BF16))
        for nm in ["pe", "act", "dve", "gp", "x", "w", "out", "sc"]:
            sems[nm] = _st.enter_context(nc.semaphore("s_" + nm))

        ub = [util_t[:, 0:1024], util_t[:, 1024:2048]]
        uf = [util_t[:, 0:1024].bitcast(F32), util_t[:, 1024:2048].bitcast(F32)]
        ub_cv = [u.rearrange("p (c j) -> p c j", c=8) for u in ub]
        P3 = P_sb[:, :].rearrange("p (b n) -> p b n", n=512)
        h1t_v = h1t_s[:, :].rearrange("p (c v j) -> p c v j", c=8, v=2)
        h2T_v = h2T[:, :].rearrange("p (c w) -> p c w", w=257)

        h1tok_b = [h1tok, h2tok2]

        util_free = [None, None]
        uidx = [0]

        import os as _os
        _single_util = _os.environ.get("KBISECT_UTIL1") == "1"
        def uacq():
            if _single_util:
                return 0
            k = uidx[0]
            uidx[0] ^= 1
            return k

        def dspace(k=1):
            for _ in range(k):
                EI("vector", lambda e: e.tensor_copy(dbg1[:, 0:1], czero[:, 0:1]), "dve")

        def pe_margin(sem="pe"):
            EI("tensor", lambda e: e.ldweights(ident[:, 0:128]), sem)
            return EI("tensor", lambda e: e.ldweights(ident[:, 0:128]), sem)

        def gp_const(g):
            g.memset(ident[:], 0.0)
            g.affine_select(out=ident[:], in_=ident[:], compare_op=ALU.not_equal,
                            fill=1.0, base=0, pattern=[[-1, 128]], channel_multiplier=1)
            g.memset(ones2[:], 0.0)
            g.memset(ones2[0:1, :], 1.0)
            g.memset(ones2[64:65, :], 1.0)
            g.memset(onesc[:], 1.0)
            g.memset(epsc[:], EPS)
            g.memset(cinv[:], 1.0 / D)
            g.memset(czero[:], 0.0)
            g.memset(cmhalf[:], -0.5)
            g.memset(sm[:], 0.0)
            g.memset(v_aug[:], 1.0)
            return g.memset(h2T[:], 0.0)
        gp_done = EI("gpsimd", gp_const, "gp")

        for j in range(8):
            EI("gpsimd", lambda g, j=j: g.dma_start(
                out=mur_c[:, j:j + 1],
                in_=mur_d[128 * j:128 * j + 128].rearrange("(p o) -> p o", o=1)), "w", 16)
            EI("gpsimd", lambda g, j=j: g.dma_start(
                out=muk_c[:, j:j + 1],
                in_=muk_d[128 * j:128 * j + 128].rearrange("(p o) -> p o", o=1)), "w", 16)
        w_small = EI("gpsimd", lambda g: g.dma_start(
            out=sc11[0:1, 0:1], in_=scl_d.rearrange("(p o) -> p o", o=1)), "w", 16)

        W("scalar", gp_done)
        W("vector", gp_done)
        W("vector", w_small)
        W("scalar", w_small)
        a_ln = EI("scalar", lambda e: e.activation(sc11[0:1, 1:2], sc11[0:1, 0:1], AF.Ln,
                                                   bias=0.0, scale=1.0), "act")
        W("vector", a_ln)
        d_ng = EI("vector", lambda e: e.tensor_scalar_mul(sc11[0:1, 1:2], sc11[0:1, 1:2],
                                                          cmhalf[0:1, :]), "dve")
        W("scalar", d_ng)
        a_sc2 = EI("scalar", lambda e: e.activation(sc11[0:1, 1:2], sc11[0:1, 1:2], AF.Exp,
                                                    bias=0.0, scale=-1.0), "act")
        W("vector", a_sc2)
        d_sc2 = EI("vector", lambda e: e.tensor_copy(smb[0:1, 0:1], sc11[0:1, 1:2]), "dve")
        W("tensor", d_sc2)
        W("tensor", gp_done)
        k0 = uacq()
        p_sc2 = EI("tensor", lambda e, k0=k0: e.matmul(uf[k0][:, 0:1], ones2[0:1, :],
                                                       smb[0:1, 0:1], start=True, stop=True), "pe")
        p_sc2 = pe_margin()
        W("vector", p_sc2)
        util_free[k0] = EI("vector", lambda e, k0=k0: e.tensor_copy(sc2col[:], uf[k0][:, 0:1]), "dve")

        def stage_weight_thunks(loads, n_ci, emit_tr, emit_ev):
            th = []
            wtok = {}
            def t_dma():
                W("gpsimd", ("pe", ctr["pe"]))
                for ld in loads:
                    wtok["w"] = EI("gpsimd", ld, "w", 16)
            th.append(t_dma)
            def mk(ci):
                def f():
                    k = uacq()
                    W("tensor", wtok["w"])
                    W("tensor", util_free[k])
                    pc = EI("tensor", lambda e, ci=ci, k=k: emit_tr(e, ci, k), "pe")
                    pc = pe_margin()
                    W("scalar", pc)
                    util_free[k] = EI("scalar", lambda e, ci=ci, k=k: emit_ev(e, ci, k), "act")
                return f
            for ci in range(n_ci):
                th.append(mk(ci))
            return th

        def big_tr(wt):
            def tr(e, ci, k):
                for oj in range(8):
                    ins = e.transpose(ub[k][:, 128 * oj:128 * oj + 128],
                                      wnat[:, 1024 * oj + 128 * ci:1024 * oj + 128 * ci + 128],
                                      ident[:])
                return ins
            def ev(e, ci, k):
                return e.activation(wt[:, 1024 * ci:1024 * ci + 1024], ub[k][:, :],
                                    AF.Copy, bias=0.0, scale=1.0)
            return tr, ev

        stage_pending = []
        for wd, wt in [(wq_d, wq_t), (wo_d, wo_t), (fr_d, fr_t), (fk_d, fk_t), (fv_d, fv_t)]:
            tr, ev = big_tr(wt)
            stage_pending += stage_weight_thunks([lambda g, wd=wd: g.dma_start(
                out=wnat.rearrange("p (a c) -> p a c", c=1024),
                in_=wd.rearrange("(a p) c -> p a c", p=128))], 8, tr, ev)

        def kv_tr(e, ci, k):
            return e.transpose(ub[k][:, 0:128], wnat[:, 128 * ci:128 * ci + 128], ident[:])
        def kv_ev(e, ci, k):
            return e.activation(wkv_t[:, 128 * ci:128 * ci + 128], ub[k][:, 0:128],
                                AF.Copy, bias=0.0, scale=1.0)
        for f in stage_weight_thunks([lambda g: g.dma_start(out=arena_t[0:64, 0:1024], in_=wv_d),
                                      lambda g: g.dma_start(out=arena_t[64:128, 0:1024], in_=wk_d)],
                                     8, kv_tr, kv_ev):
            f()

        def emit_ln(src_ap_fn, ready_dve, ready_act, o=0):
            c = 8 * o
            W("vector", ready_dve)
            EI("vector", lambda e, c=c: e.reduce_sum(stt[:, c:c + 1], src_ap_fn(), axis=AX.X), "dve")
            dspace()
            dm = EI("vector", lambda e, c=c: e.tensor_scalar_mul(stt[:, c + 5:c + 6],
                                                                 stt[:, c:c + 1], cinv[:]), "dve")
            dspace()
            W("scalar", ready_act)
            asq = EI("scalar", lambda e, o=o: e.activation(junk[:, :], src_ap_fn(), AF.Square,
                                                           bias=0.0, scale=1.0,
                                                           accum_out=sacc[:, o:o + 1]), "act")
            W("vector", asq)
            EI("vector", lambda e, c=c: e.tensor_mul(stt[:, c + 2:c + 3], stt[:, c + 5:c + 6],
                                                     stt[:, c + 5:c + 6]), "dve")
            dspace()
            EI("vector", lambda e, c=c, o=o: e.tensor_scalar(stt[:, c + 3:c + 4], sacc[:, o:o + 1],
                                                             cinv[:], stt[:, c + 2:c + 3],
                                                             ALU.mult, ALU.subtract), "dve")
            dspace()
            dvar = ("dve", ctr["dve"])
            W("scalar", dvar)
            aln = EI("scalar", lambda e, c=c: e.activation(stt[:, c + 4:c + 5], stt[:, c + 3:c + 4],
                                                           AF.Ln, bias=epsc[:], scale=1.0), "act")
            W("vector", aln)
            EI("vector", lambda e, c=c: e.tensor_scalar_mul(stt[:, c + 7:c + 8], stt[:, c + 4:c + 5],
                                                            cmhalf[:]), "dve")
            dspace()
            dng = ("dve", ctr["dve"])
            W("scalar", dng)
            arstd = EI("scalar", lambda e, c=c: e.activation(stt[:, c + 6:c + 7], stt[:, c + 7:c + 8],
                                                             AF.Exp, bias=0.0, scale=1.0), "act")
            return dm, arstd

        h1t_sb = [h1t_s, P_sb]
        h1t_vb = [h1t_s[:, :].rearrange("p (c v j) -> p c v j", c=8, v=2),
                  P_sb[:, :].rearrange("p (c v j) -> p c v j", c=8, v=2)]
        ln1_dve = [None] * NBLK
        ln1_act = [None] * NBLK
        kmax_done = None

        def pa_head(b):
            slot = b % 3
            if b >= 3:
                W("sync", ln1_dve[b - 3])
                W("sync", ln1_act[b - 3])
            xcnt = EI("sync", lambda e, b=b, slot=slot: e.dma_start(
                out=xt[:, 2048 * slot:2048 * slot + 2048].rearrange("p (a c) -> p a c", c=1024),
                in_=x_t[:, 2 * b:2 * b + 2, :]), "x", 16)
            for u in range(2):
                t = 2 * b + u
                xcol = 2048 * slot + 1024 * u
                c = 8 * u
                dm, arstd = emit_ln(lambda xcol=xcol: xt[:, xcol:xcol + 1024], xcnt, xcnt, o=u)
                ln1_act[b] = arstd
                W("vector", arstd)
                dspace()
                EI("vector", lambda e, t=t, c=c: e.tensor_copy(st1[:, 2 * t:2 * t + 1],
                                                               stt[:, c + 5:c + 6]), "dve")
                EI("vector", lambda e, t=t, c=c: e.tensor_copy(st1[:, 2 * t + 1:2 * t + 2],
                                                               stt[:, c + 6:c + 7]), "dve")
                EI("vector", lambda e, xcol=xcol, u=u, c=c: e.tensor_scalar(
                    h1tok_b[u][:, :], xt[:, xcol:xcol + 1024], stt[:, c + 5:c + 6],
                    stt[:, c + 6:c + 7], ALU.subtract, ALU.mult), "dve")
                dspace()
                dh1 = ("dve", ctr["dve"])
                ln1_dve[b] = dh1
                k = uacq()
                W("tensor", dh1)
                W("tensor", util_free[k])
                def trh(e, k=k, u=u):
                    for ci in range(8):
                        ins = e.transpose(ub[k][:, 128 * ci:128 * ci + 128],
                                          h1tok_b[u][:, 128 * ci:128 * ci + 128], ident[:])
                    return ins
                pc = EI("tensor", trh, "pe")
                pc = pe_margin()
                W("scalar", pc)
                util_free[k] = EI("scalar", lambda e, u=u, k=k, b=b: e.activation(
                    h1t_vb[b % 2][:, :, u, :], ub_cv[k],
                    AF.Copy, bias=0.0, scale=1.0), "act")

        def pa_tail(b):
            nonlocal kmax_done
            W("tensor", util_free[uidx[0] ^ 1])
            W("tensor", util_free[uidx[0]])
            EI("tensor", lambda e: e.ldweights(ident[:, 0:128]), "pe")
            def kvmm(e, b=b):
                for ci in range(8):
                    e.matmul(ring[:, 0:256], wkv_t[:, 128 * ci:128 * ci + 128],
                             h1t_sb[b % 2][:, 256 * ci:256 * ci + 256], start=(ci == 0),
                             stop=(ci == 7), skip_group_check=True)
                for ci in range(8):
                    ins = e.matmul(ring[0:64, 512:768], wkv_t[:, 128 * ci + 64:128 * ci + 128],
                                   h1t_sb[b % 2][:, 256 * ci:256 * ci + 256], start=(ci == 0),
                                   stop=(ci == 7), skip_group_check=True)
                return ins
            pkv = EI("tensor", kvmm, "pe")
            pkv = pe_margin()
            W("vector", pkv)
            EI("vector", lambda e, b=b: e.tensor_scalar_mul(
                kTs2[64:128, 256 * b:256 * b + 256], ring[64:128, 0:256], sc2col[64:128, :]), "dve")
            EI("vector", lambda e, b=b: e.tensor_scalar_mul(
                kTs2[0:64, 256 * b:256 * b + 256], ring[0:64, 512:768], sc2col[0:64, :]), "dve")
            EI("vector", lambda e: e.tensor_copy(vT_tmp[0:64, 0:256], ring[0:64, 0:256]), "dve")
            dspace()
            dkv = EI("vector", lambda e, b=b: e.tensor_mul(
                ksq[64:128, 0:256], kTs2[64:128, 256 * b:256 * b + 256],
                kTs2[64:128, 256 * b:256 * b + 256]), "dve")
            k = uacq()
            W("tensor", dkv)
            W("tensor", util_free[k])
            def vtr(e, k=k):
                for u2 in range(2):
                    e.transpose(ub[k][:, 64 * u2:64 * u2 + 64],
                                vT_tmp[0:64, 128 * u2:128 * u2 + 128], ident[0:64, 0:64])
                return e.matmul(uf[k][0:1, 256:512], onesc[64:128, 0:1], ksq[64:128, 0:256],
                                start=True, stop=True, skip_group_check=True)
            pvt = EI("tensor", vtr, "pe")
            pvt = pe_margin()
            W("vector", pvt)
            EI("vector", lambda e, b=b, k=k: e.tensor_copy(
                v_aug[:, :].rearrange("p (t j) -> p t j", j=65)[:, 2 * b:2 * b + 2, 0:64],
                ub[k][:, 0:128].rearrange("p (t j) -> p t j", j=64)), "dve")
            EI("vector", lambda e, k=k: e.reduce_max(sm[0:1, 1:2], uf[k][0:1, 256:512], axis=AX.X), "dve")
            dspace()
            util_free[k] = EI("vector", lambda e: e.tensor_max(sm[0:1, 0:1], sm[0:1, 0:1],
                                                               sm[0:1, 1:2]), "dve")
            kmax_done = util_free[k]

        for b in range(NBLK):
            if b >= 1:
                pa_tail(b - 1)
            pa_head(b)
            for _ in range(3):
                if stage_pending:
                    stage_pending.pop(0)()
        pa_tail(NBLK - 1)
        while stage_pending:
            stage_pending.pop(0)()

        bank_free_act = [None] * 4
        fin_tok = [None, None]
        fin_dve = None
        x_slot_dve = {13 % 3: ln1_dve[13], 14 % 3: ln1_dve[14], 15 % 3: ln1_dve[15]}
        x_slot_act = {13 % 3: ln1_act[13], 14 % 3: ln1_act[14], 15 % 3: ln1_act[15]}
        xb_cnt = [None] * (NBLK + 2)
        out_cnt = [None] * (NBLK + 2)
        bias_done = [None] * (NBLK + 1)

        NG = 16

        def emit_xdma(i):
            if i >= n_sblk:
                return
            slot = i % 3
            W("sync", x_slot_dve.get(slot))
            W("sync", x_slot_act.get(slot))
            xb_cnt[i] = EI("sync", lambda e, i=i, slot=slot: e.dma_start(
                out=xt[:, 2048 * slot:2048 * slot + 2048].rearrange("p (a c) -> p a c", c=1024),
                in_=x_t[:, 2 * i:2 * i + 2, :]), "x", 16)

        def b0_thunks(i):
            if i >= n_sblk:
                return []
            th = []
            par = i % 2
            xbase = 2048 * (i % 3)
            qb = 2048 * par
            cell = {}

            def t_h1a(u):
                def f():
                    t = 2 * i + u
                    xcol = xbase + 1024 * u
                    W("vector", xb_cnt[i])
                    if u == 1:
                        W("vector", cell.get("trh1_0"))
                    EI("vector", lambda e, xcol=xcol, t=t: e.tensor_scalar(
                        h1tok[:, :], xt[:, xcol:xcol + 1024], st1[:, 2 * t:2 * t + 1],
                        st1[:, 2 * t + 1:2 * t + 2], ALU.subtract, ALU.mult), "dve")
                    dspace()
                    cell["dh1"] = ("dve", ctr["dve"])
                return f

            def t_h1b(u):
                def f():
                    k = uacq()
                    W("tensor", cell["dh1"])
                    W("tensor", util_free[k])
                    def trh(e, k=k):
                        for ci in range(8):
                            ins = e.transpose(ub[k][:, 128 * ci:128 * ci + 128],
                                              h1tok[:, 128 * ci:128 * ci + 128], ident[:])
                        return ins
                    pc = EI("tensor", trh, "pe")
                    pc = pe_margin()
                    cell["trh1_%d" % u] = pc
                    W("vector", pc)
                    util_free[k] = EI("vector", lambda e, u=u, k=k: e.tensor_copy(
                        h1t_v[:, :, u, :], ub_cv[k]), "dve")
                    dspace()
                    cell["h1ev%d" % u] = ("dve", ctr["dve"])
                return f

            def t_q(j):
                def f():
                    k = uacq()
                    W("tensor", util_free[k])
                    if j == 0:
                        W("tensor", tmax(cell["h1ev0"], cell["h1ev1"]))
                    def qmm(e, j=j, k=k):
                        for ci in range(8):
                            ins = e.matmul(uf[k][:, 0:256],
                                           wq_t[:, 1024 * ci + 128 * j:1024 * ci + 128 * j + 128],
                                           h1t_s[:, 256 * ci:256 * ci + 256],
                                           start=(ci == 0), stop=(ci == 7), skip_group_check=True)
                        if j >= 2:
                            ins = e.matmul(uf[k][0:1, 256:512], onesc[:, 0:1],
                                           qsq[:, 256 * (j % 2):256 * (j % 2) + 256],
                                           start=False, stop=True, skip_group_check=True)
                        return ins
                    pq = EI("tensor", qmm, "pe")
                    pq = pe_margin()
                    W("vector", pq)
                    EI("vector", lambda e, j=j, k=k: e.tensor_scalar_mul(
                        qT_s[:, qb + 256 * j:qb + 256 * j + 256], uf[k][:, 0:256], sc2col[:]), "dve")
                    dspace()
                    EI("vector", lambda e, j=j: e.tensor_mul(
                        qsq[:, 256 * (j % 2):256 * (j % 2) + 256],
                        qT_s[:, qb + 256 * j:qb + 256 * j + 256],
                        qT_s[:, qb + 256 * j:qb + 256 * j + 256]), "dve")
                    if j == 2:
                        EI("vector", lambda e, k=k: e.tensor_copy(
                            nacc[0:1, 256 * par:256 * par + 256], uf[k][0:1, 256:512]), "dve")
                    elif j > 2:
                        EI("vector", lambda e, k=k: e.tensor_max(
                            nacc[0:1, 256 * par:256 * par + 256],
                            nacc[0:1, 256 * par:256 * par + 256], uf[k][0:1, 256:512]), "dve")
                    util_free[k] = ("dve", ctr["dve"])
                return f

            def t_bound_a():
                def f():
                    toks = []
                    for jj in (6, 7):
                        k = uacq()
                        W("tensor", util_free[k])
                        EI("tensor", lambda e, jj=jj, k=k: e.matmul(
                            uf[k][0:1, 256:512], onesc[:, 0:1],
                            qsq[:, 256 * (jj % 2):256 * (jj % 2) + 256],
                            start=True, stop=True, skip_group_check=True), "pe")
                        toks.append((k, pe_margin()))
                    for k, tok in toks:
                        W("vector", tok)
                        util_free[k] = EI("vector", lambda e, k=k: e.tensor_max(
                            nacc[0:1, 256 * par:256 * par + 256],
                            nacc[0:1, 256 * par:256 * par + 256], uf[k][0:1, 256:512]), "dve")
                    dspace()
                    EI("vector", lambda e: e.reduce_max(sm[0:1, 2 + 4 * par:3 + 4 * par],
                                                        nacc[0:1, 256 * par:256 * par + 256],
                                                        axis=AX.X), "dve")
                    dspace()
                    W("vector", kmax_done)
                    EI("vector", lambda e: e.tensor_add(sm[0:1, 3 + 4 * par:4 + 4 * par],
                                                        sm[0:1, 2 + 4 * par:3 + 4 * par],
                                                        sm[0:1, 0:1]), "dve")
                    dspace()
                    EI("vector", lambda e: e.tensor_scalar_mul(
                        sm[0:1, 4 + 4 * par:5 + 4 * par], sm[0:1, 3 + 4 * par:4 + 4 * par],
                        cmhalf[0:1, :]), "dve")
                    W("vector", ("act", ctr["act"]))
                    dspace()
                    cell["dbias"] = EI("vector", lambda e: e.tensor_copy(
                        smb[0:1, 1 + par:2 + par], sm[0:1, 4 + 4 * par:5 + 4 * par]), "dve")
                return f

            def t_bound_b():
                def f():
                    k = uacq()
                    W("tensor", cell["dbias"])
                    W("tensor", util_free[k])
                    EI("tensor", lambda e, k=k: e.matmul(uf[k][:, 0:1], ones2[0:1, :],
                                                         smb[0:1, 1 + par:2 + par],
                                                         start=True, stop=True,
                                                         skip_group_check=True), "pe")
                    pb = pe_margin()
                    W("vector", pb)
                    util_free[k] = EI("vector", lambda e, k=k: e.tensor_copy(
                        bias_col[:, par:par + 1], uf[k][:, 0:1]), "dve")
                    dspace()
                    bias_done[i] = ("dve", ctr["dve"])
                return f

            th.append(t_h1a(0))
            th.append("gap")
            th.append(t_h1b(0))
            th.append(t_h1a(1))
            th.append("gap")
            th.append(t_h1b(1))
            for j in range(8):
                th.append(t_q(j))
            th.append(t_bound_a())
            th.append("gap")
            th.append(t_bound_b())
            return th

        h2tok_b = [h2tok, h2tok2]

        def b2_thunks(i):
            if i < 0 or i >= n_sblk:
                return []
            th = []
            par = i % 2
            xbase = 2048 * (i % 3)
            ab = 2048 * par
            cell = {}

            def t_wo(s_sub, half):
                def f():
                    nonlocal fin_dve
                    k = uacq()
                    W("vector", out_cnt[i])
                    W("tensor", util_free[k])
                    W("tensor", fin_dve)
                    def womm(e, s_sub=s_sub, half=half, k=k):
                        for ci in range(8):
                            ins = e.matmul(
                                uf[k][:, 0:512],
                                attnT[:, ab + 256 * ci + 128 * s_sub:ab + 256 * ci + 128 * s_sub + 128],
                                wo_t[:, 1024 * ci + 512 * half:1024 * ci + 512 * half + 512],
                                start=(ci == 0), stop=(ci == 7), skip_group_check=True)
                        return ins
                    pw = EI("tensor", womm, "pe")
                    pw = pe_margin()
                    W("vector", pw)
                    util_free[k] = EI("vector", lambda e, s_sub=s_sub, half=half, k=k: e.tensor_copy(
                        att_o[:, 1024 * s_sub + 512 * half:1024 * s_sub + 512 * half + 512],
                        uf[k][:, 0:512]), "dve")
                    cell["ev_%d_%d" % (s_sub, half)] = util_free[k]
                return f

            def t_ln_a(s_sub):
                def f():
                    xcol = xbase + 1024 * s_sub
                    ac0 = 1024 * s_sub
                    c = 8 * s_sub
                    c2 = 16 + 8 * s_sub
                    o2 = 4 + s_sub
                    EI("vector", lambda e, xcol=xcol, c2=c2: e.reduce_sum(
                        stt[:, c2:c2 + 1], xt[:, xcol:xcol + 1024], axis=AX.X), "dve")
                    dspace()
                    EI("vector", lambda e, c2=c2: e.tensor_scalar_mul(
                        stt[:, c2 + 5:c2 + 6], stt[:, c2:c2 + 1], cinv[:]), "dve")
                    dspace()
                    EI("vector", lambda e, c2=c2: e.tensor_mul(
                        stt[:, c2 + 2:c2 + 3], stt[:, c2 + 5:c2 + 6], stt[:, c2 + 5:c2 + 6]), "dve")
                    dspace()
                    dm, arstd = emit_ln(lambda ac0=ac0: att_o[:, ac0:ac0 + 1024],
                                        None,
                                        tmax(cell["ev_%d_0" % s_sub], cell["ev_%d_1" % s_sub]),
                                        o=s_sub)
                    W("vector", arstd)
                    dspace()
                    EI("vector", lambda e, ac0=ac0, s_sub=s_sub, c=c: e.tensor_scalar(
                        out_sb[:, 1024 * s_sub:1024 * s_sub + 1024], att_o[:, ac0:ac0 + 1024],
                        stt[:, c + 5:c + 6], stt[:, c + 6:c + 7], ALU.subtract, ALU.mult), "dve")
                    dspace()
                    EI("vector", lambda e, xcol=xcol, s_sub=s_sub: e.tensor_add(
                        xt[:, xcol:xcol + 1024], xt[:, xcol:xcol + 1024],
                        out_sb[:, 1024 * s_sub:1024 * s_sub + 1024]), "dve")
                    dspace()
                    dadd2 = ("dve", ctr["dve"])
                    W("scalar", dadd2)
                    asq2 = EI("scalar", lambda e, xcol=xcol, o2=o2: e.activation(
                        junk[:, :], xt[:, xcol:xcol + 1024], AF.Square, bias=0.0, scale=1.0,
                        accum_out=sacc[:, o2:o2 + 1]), "act")
                    W("vector", asq2)
                    EI("vector", lambda e, c2=c2, o2=o2: e.tensor_scalar(
                        stt[:, c2 + 3:c2 + 4], sacc[:, o2:o2 + 1], cinv[:],
                        stt[:, c2 + 2:c2 + 3], ALU.mult, ALU.subtract), "dve")
                    dspace()
                    dvar2 = ("dve", ctr["dve"])
                    W("scalar", dvar2)
                    aln2 = EI("scalar", lambda e, c2=c2: e.activation(
                        stt[:, c2 + 4:c2 + 5], stt[:, c2 + 3:c2 + 4], AF.Ln,
                        bias=epsc[:], scale=1.0), "act")
                    W("vector", aln2)
                    EI("vector", lambda e, c2=c2: e.tensor_scalar_mul(
                        stt[:, c2 + 7:c2 + 8], stt[:, c2 + 4:c2 + 5], cmhalf[:]), "dve")
                    dspace()
                    dng2 = ("dve", ctr["dve"])
                    W("scalar", dng2)
                    arstd2 = EI("scalar", lambda e, c2=c2: e.activation(
                        stt[:, c2 + 6:c2 + 7], stt[:, c2 + 7:c2 + 8], AF.Exp,
                        bias=0.0, scale=1.0), "act")
                    W("vector", arstd2)
                    dspace()
                    EI("vector", lambda e, xcol=xcol, s_sub=s_sub, c2=c2: e.tensor_scalar(
                        h2tok_b[s_sub][:, :],
                        xt[:, xcol:xcol + 1024], stt[:, c2 + 5:c2 + 6], stt[:, c2 + 6:c2 + 7],
                        ALU.subtract, ALU.mult), "dve")
                    dspace()
                    cell["dh2_%d" % s_sub] = ("dve", ctr["dve"])
                return f

            def t_ln_b(s_sub):
                def f():
                    k = uacq()
                    W("tensor", cell["dh2_%d" % s_sub])
                    W("tensor", util_free[k])
                    def trh2(e, k=k, s_sub=s_sub):
                        for ci in range(8):
                            ins = e.transpose(ub[k][:, 128 * ci:128 * ci + 128],
                                              h2tok_b[s_sub][:, 128 * ci:128 * ci + 128], ident[:])
                        return ins
                    pc = EI("tensor", trh2, "pe")
                    pc = pe_margin()
                    cell["trh2_%d" % s_sub] = pc
                    W("vector", pc)
                    util_free[k] = EI("vector", lambda e, s_sub=s_sub, k=k: e.tensor_copy(
                        h2T_v[:, :, 1 + 128 * s_sub:1 + 128 * s_sub + 128], ub_cv[k]), "dve")
                return f

            def t_mix():
                def f():
                    for ci in range(8):
                        c0 = 257 * ci
                        EI("vector", lambda e, c0=c0, ci=ci: e.tensor_sub(
                            diffT[:, 256 * ci:256 * ci + 256], h2T[:, c0 + 1:c0 + 257],
                            h2T[:, c0:c0 + 256]), "dve")
                    for ci in range(8):
                        c0 = 257 * ci
                        EI("vector", lambda e, c0=c0, ci=ci: e.scalar_tensor_tensor(
                            xkT[:, 256 * ci:256 * ci + 256], diffT[:, 256 * ci:256 * ci + 256],
                            muk_c[:, ci:ci + 1], h2T[:, c0:c0 + 256], ALU.mult, ALU.add), "dve")
                    cell["mix"] = ("dve", ctr["dve"])
                    for ci in range(8):
                        c0 = 257 * ci
                        EI("vector", lambda e, c0=c0, ci=ci: e.scalar_tensor_tensor(
                            xrT[:, 256 * ci:256 * ci + 256], diffT[:, 256 * ci:256 * ci + 256],
                            mur_c[:, ci:ci + 1], h2T[:, c0:c0 + 256], ALU.mult, ALU.add), "dve")
                    for ci in range(8):
                        c0 = 257 * ci
                        EI("vector", lambda e, c0=c0: e.tensor_copy(
                            h2T[:, c0:c0 + 1], h2T[:, c0 + 256:c0 + 257]), "dve")
                return f

            def t_fk(j):
                def f():
                    k = uacq()
                    W("tensor", util_free[k])
                    W("tensor", cell["mix"])
                    def fkmm(e, j=j, k=k):
                        for ci in range(8):
                            ins = e.matmul(uf[k][:, 0:256],
                                           fk_t[:, 1024 * ci + 128 * j:1024 * ci + 128 * j + 128],
                                           xkT[:, 256 * ci:256 * ci + 256],
                                           start=(ci == 0), stop=(ci == 7), skip_group_check=True)
                        return ins
                    pk = EI("tensor", fkmm, "pe")
                    pk = pe_margin()
                    W("vector", pk)
                    EI("vector", lambda e, k=k: e.tensor_scalar_max(katmp[:, 0:256],
                                                                    uf[k][:, 0:256], czero[:]), "dve")
                    dspace()
                    util_free[k] = EI("vector", lambda e, j=j: e.tensor_mul(
                        kaT[:, 256 * j:256 * j + 256], katmp[:, 0:256], katmp[:, 0:256]), "dve")
                return f

            def t_fr(j):
                def f():
                    k = uacq()
                    W("tensor", util_free[k])
                    def frmm(e, j=j, k=k):
                        for ci in range(8):
                            ins = e.matmul(uf[k][:, 0:256],
                                           fr_t[:, 1024 * ci + 128 * j:1024 * ci + 128 * j + 128],
                                           xrT[:, 256 * ci:256 * ci + 256],
                                           start=(ci == 0), stop=(ci == 7), skip_group_check=True)
                        return ins
                    pr = EI("tensor", frmm, "pe")
                    pr = pe_margin()
                    W("vector", pr)
                    util_free[k] = EI("vector", lambda e, j=j, k=k: e.tensor_copy(
                        diffT[:, 256 * j:256 * j + 256], uf[k][:, 0:256]), "dve")
                    if j == 7:
                        W("scalar", util_free[k])
                        cell["sig"] = EI("scalar", lambda e: e.activation(
                            rT[:, 0:2048], diffT[:, 0:2048], AF.Sigmoid,
                            bias=0.0, scale=1.0), "act")
                return f

            def t_fv(j):
                def f():
                    k = uacq()
                    W("tensor", util_free[k])
                    if j == 0:
                        W("tensor", cell["sig"])
                        pe_margin()
                    def fvmm(e, j=j, k=k):
                        for ci in range(8):
                            ins = e.matmul(uf[k][:, 0:256],
                                           fv_t[:, 1024 * ci + 128 * j:1024 * ci + 128 * j + 128],
                                           kaT[:, 256 * ci:256 * ci + 256],
                                           start=(ci == 0), stop=(ci == 7), skip_group_check=True)
                        return ins
                    pv = EI("tensor", fvmm, "pe")
                    pv = pe_margin()
                    W("vector", pv)
                    util_free[k] = EI("vector", lambda e, j=j, k=k: e.tensor_mul(
                        ffT[:, 256 * j:256 * j + 256], uf[k][:, 0:256],
                        rT[:, 256 * j:256 * j + 256]), "dve")
                    if j == 7:
                        cell["ff"] = util_free[k]
                return f

            def t_out_a(s_sub):
                def f():
                    k = uacq()
                    W("tensor", util_free[k])
                    W("tensor", cell["ff"])
                    def trf(e, s_sub=s_sub, k=k):
                        for ci in range(8):
                            ins = e.transpose(ub[k][:, 128 * ci:128 * ci + 128],
                                              ffT[:, 256 * ci + 128 * s_sub:256 * ci + 128 * s_sub + 128],
                                              ident[:])
                        return ins
                    pf = EI("tensor", trf, "pe")
                    cell["pf_%d" % s_sub] = pe_margin()
                    cell["uk_%d" % s_sub] = k
                return f

            def t_out_b(s_sub):
                def f():
                    k = cell["uk_%d" % s_sub]
                    pf = cell["pf_%d" % s_sub]
                    xcol = xbase + 1024 * s_sub
                    c = 8 * s_sub
                    o = 2 + s_sub
                    W("vector", pf)
                    EI("vector", lambda e, k=k, c=c: e.reduce_sum(stt[:, c:c + 1], ub[k][:, :],
                                                                  axis=AX.X), "dve")
                    dspace()
                    EI("vector", lambda e, c=c: e.tensor_scalar_mul(stt[:, c + 5:c + 6],
                                                                    stt[:, c:c + 1], cinv[:]), "dve")
                    dspace()
                    dm3 = ("dve", ctr["dve"])
                    W("scalar", dm3)
                    asq = EI("scalar", lambda e, k=k, o=o: e.activation(junk[:, :], ub[k][:, :],
                                                                        AF.Square, bias=0.0, scale=1.0,
                                                                        accum_out=sacc[:, o:o + 1]), "act")
                    W("vector", asq)
                    EI("vector", lambda e, c=c: e.tensor_mul(stt[:, c + 2:c + 3], stt[:, c + 5:c + 6],
                                                             stt[:, c + 5:c + 6]), "dve")
                    dspace()
                    EI("vector", lambda e, c=c, o=o: e.tensor_scalar(stt[:, c + 3:c + 4],
                                                                     sacc[:, o:o + 1], cinv[:],
                                                                     stt[:, c + 2:c + 3], ALU.mult,
                                                                     ALU.subtract), "dve")
                    dspace()
                    dvar = ("dve", ctr["dve"])
                    W("scalar", dvar)
                    aln = EI("scalar", lambda e, c=c: e.activation(stt[:, c + 4:c + 5],
                                                                   stt[:, c + 3:c + 4], AF.Ln,
                                                                   bias=epsc[:], scale=1.0), "act")
                    W("vector", aln)
                    EI("vector", lambda e, c=c: e.tensor_scalar_mul(stt[:, c + 7:c + 8],
                                                                    stt[:, c + 4:c + 5],
                                                                    cmhalf[:]), "dve")
                    dspace()
                    dng = ("dve", ctr["dve"])
                    W("scalar", dng)
                    arstd = EI("scalar", lambda e, c=c: e.activation(stt[:, c + 6:c + 7],
                                                                     stt[:, c + 7:c + 8], AF.Exp,
                                                                     bias=0.0, scale=1.0), "act")
                    W("vector", arstd)
                    dspace()
                    util_free[k] = EI("vector", lambda e, s_sub=s_sub, k=k, c=c: e.tensor_scalar(
                        out_sb[:, 1024 * s_sub:1024 * s_sub + 1024], ub[k][:, :],
                        stt[:, c + 5:c + 6], stt[:, c + 6:c + 7], ALU.subtract, ALU.mult), "dve")
                    dspace()
                    dadd = EI("vector", lambda e, xcol=xcol, s_sub=s_sub: e.tensor_add(
                        out_sb[:, 1024 * s_sub:1024 * s_sub + 1024],
                        out_sb[:, 1024 * s_sub:1024 * s_sub + 1024],
                        xt[:, xcol:xcol + 1024]), "dve")
                    if s_sub == 1:
                        x_slot_dve[i % 3] = dadd
                        x_slot_act[i % 3] = ("act", ctr["act"])
                        W("sync", dadd)
                        out_cnt[i + 1] = EI("sync", lambda e, i=i: e.dma_start(
                            out=y_t[:, 2 * i:2 * i + 2, :],
                            in_=out_sb[:, :].rearrange("p (a c) -> p a c", c=1024)), "out", 16)
                return f

            th.append("gap")
            th.append("gap")
            for s_sub in range(2):
                for half in range(2):
                    th.append(t_wo(s_sub, half))
            th.append(t_ln_a(0))
            th.append(t_ln_a(1))
            th.append("gap")
            th.append("gap")
            th.append("gap")
            th.append(t_ln_b(0))
            th.append("gap")
            th.append(t_ln_b(1))
            th.append(t_mix())
            th.append("gap")
            for j in range(8):
                th.append(t_fk(j))
            for j in range(8):
                th.append(t_fr(j))
            for j in range(8):
                th.append(t_fv(j))
            th.append(t_out_a(0))
            th.append(t_out_a(1))
            th.append("gap")
            th.append(t_out_b(0))
            th.append("gap")
            th.append(t_out_b(1))
            th.append("gap")
            th.append("gap")
            return th

        fin_pending = []

        def spine(i, pending):
            nonlocal fin_dve
            par = i % 2
            qb = 2048 * par
            ab = 2048 * par
            n_slots = 8 * NG
            n0 = len(pending)
            done = [0]
            slot = [0]

            def inject():
                slot[0] += 1
                if fin_pending:
                    fe = fin_pending.pop(0)
                    if fe != "gap":
                        fe()
                target = (n0 * slot[0]) // n_slots
                while pending and done[0] < target:
                    pe_ = pending.pop(0)
                    if pe_ != "gap":
                        pe_()
                    done[0] += 1

            NHPS = 8
            TOT = NHPS * NG

            def hp_of(G):
                return G // NG

            def emit_scores(G):
                g = G % NG
                hp = hp_of(G)
                fb = 2 * (G % 2)
                t0 = 2 * g
                W("tensor", tmax(bank_free_act[fb], bank_free_act[fb + 1]))
                def sc(e, t0=t0, fb=fb, hp=hp):
                    for u in range(2):
                        t = t0 + u
                        off = 256 * u
                        e.matmul(ring[:, 512 * fb + off:512 * fb + off + 256],
                                 kTs2[0:64, 128 * t:128 * t + 128],
                                 qT_s[0:64, qb + 256 * hp:qb + 256 * hp + 256],
                                 start=True, stop=True, tile_position=(0, 0),
                                 skip_group_check=True)
                        ins = e.matmul(ring[:, 512 * (fb + 1) + off:512 * (fb + 1) + off + 256],
                                       kTs2[64:128, 128 * t:128 * t + 128],
                                       qT_s[64:128, qb + 256 * hp:qb + 256 * hp + 256],
                                       start=True, stop=True, tile_position=(64, 0),
                                       skip_group_check=True)
                    return ins
                sc_tok[G] = EI("tensor", sc, "sc")
                return sc_tok[G]

            exp_tok = {}
            sc_tok = {}
            av_tok = {}

            def emit_exp(G):
                fb = 2 * (G % 2)
                W("scalar", ("sc", ctr["sc"]))
                W("scalar", bias_done[i])
                ac = EI("scalar", lambda e, fb=fb: e.activation(
                    P_sb[:, 512 * fb:512 * fb + 1024],
                    ring[:, 512 * fb:512 * fb + 1024],
                    AF.Exp, bias=bias_col[:, par:par + 1], scale=1.0), "act")
                exp_tok[G] = ac
                bank_free_act[fb] = ac
                bank_free_act[fb + 1] = ac

            def emit_attnv(G):
                g = G % NG
                hp = hp_of(G)
                apar = hp % 2
                accp = acc2[:, 512 * apar:512 * apar + 512]
                fb = 2 * (G % 2)
                t0 = 2 * g
                W("tensor", exp_tok.pop(G))
                if t0 == 0:
                    W("tensor", fin_tok[apar])
                EI("tensor", lambda e: e.ldweights(ident[:, 0:128]), "sc")
                def av(e, t0=t0, fb=fb, accp=accp):
                    for u in range(2):
                        t = t0 + u
                        off = 256 * u
                        ins = e.matmul(accp[0:65, 0:512], v_aug[:, 65 * t:65 * t + 65],
                                       P3[:, fb:fb + 2, off:off + 256],
                                       start=(t == 0), stop=(t == 31),
                                       skip_group_check=True)
                    return ins
                av_tok[G] = EI("tensor", av, "sc")
                return av_tok[G]

            def emit_fin(hp):
                nonlocal fin_dve
                apar = hp % 2
                accp = acc2[:, 512 * apar:512 * apar + 512]
                pav = pe_margin("sc")
                W("vector", pav)
                def _recip(e, accp=accp):
                    with nc.allow_low_precision(reason="per-head softmax scale; LN-invariant"):
                        return e.reciprocal(rrow[64:65, 0:512], accp[64:65, 0:512])
                EI("vector", _recip, "dve")
                EI("vector", lambda e, accp=accp: e.tensor_copy(
                    accA[0:64, 0:512], accp[0:64, 0:512]), "dve")
                if dbg and i == 0 and hp == 0:
                    EI("vector", lambda e, accp=accp: e.tensor_copy(
                        dbg2[0:65, 0:512], accp[0:65, 0:512]), "dve")
                dspace()
                fd1 = ("dve", ctr["dve"])

                def F_b(fd1=fd1, accp=accp):
                    W("tensor", fd1)
                    EI("tensor", lambda e: e.matmul(accp[0:64, 0:512], ones2[64:65, 0:64],
                                                    rrow[64:65, 0:512], start=True, stop=True,
                                                    skip_group_check=True), "sc")
                    fin_pending_tok["fp1"] = pe_margin("sc")

                def F_c(hp=hp, accp=accp):
                    W("vector", fin_pending_tok["fp1"])
                    EI("vector", lambda e, accp=accp: e.tensor_copy(
                        recipb[0:64, 0:512], accp[0:64, 0:512]), "dve")
                    dspace()
                    EI("vector", lambda e, hp=hp: e.tensor_mul(
                        attnT[0:64, ab + 256 * hp:ab + 256 * hp + 256], accA[0:64, 0:256],
                        recipb[0:64, 0:256]), "dve")
                    EI("vector", lambda e: e.tensor_mul(
                        normB[0:64, 0:256], accA[0:64, 256:512], recipb[0:64, 256:512]), "dve")
                    dspace()
                    fin_pending_tok["fd2"] = ("dve", ctr["dve"])

                def F_d(accp=accp):
                    W("tensor", fin_pending_tok["fd2"])
                    EI("tensor", lambda e, accp=accp: e.matmul(
                        accp[64:128, 0:256], ident[0:64, 0:64], normB[0:64, 0:256],
                        start=True, stop=True, tile_position=(0, 64),
                        skip_group_check=True), "sc")
                    fin_pending_tok["fp2"] = pe_margin("sc")

                def F_e(hp=hp, apar=apar, accp=accp, i=i):
                    nonlocal fin_dve
                    W("vector", fin_pending_tok["fp2"])
                    EI("vector", lambda e, hp=hp, accp=accp: e.tensor_copy(
                        attnT[64:128, ab + 256 * hp:ab + 256 * hp + 256],
                        accp[64:128, 0:256]), "dve")
                    dspace()
                    tok = ("dve", ctr["dve"])
                    if i == 0 and hp == 0 and dbg:
                        tok = EI("vector", lambda e: e.tensor_copy(
                            dbg2[:, 512:768], attnT[0:128, 0:256]), "dve")
                    fin_tok[apar] = tok
                    fin_dve = tok

                fin_pending.extend(["gap", "gap", "gap", "gap", F_b, F_c, "gap", F_d, "gap", F_e])

            for G in range(TOT):
                if G >= 1:
                    emit_exp(G - 1)
                emit_scores(G)
                if G >= 2:
                    emit_attnv(G - 2)
                    if (G - 2) % NG == NG - 1:
                        emit_fin(hp_of(G - 2))
                inject()
            emit_exp(TOT - 1)
            emit_attnv(TOT - 2)
            emit_attnv(TOT - 1)
            emit_fin(NHPS - 1)
            while fin_pending:
                fe = fin_pending.pop(0)
                if fe != "gap":
                    fe()
            while pending:
                pe_ = pending.pop(0)
                if pe_ != "gap":
                    pe_()

        fin_pending_tok = {}

        emit_xdma(0)
        for f in b0_thunks(0):
            if f != "gap":
                f()
        emit_xdma(1)
        for i in range(n_sblk):
            pending = b2_thunks(i - 1) + b0_thunks(i + 1)
            spine(i, pending)
            emit_xdma(i + 2)
        for f in b2_thunks(n_sblk - 1):
            if f != "gap":
                f()

        if dbg:
            dbg_list = [("st1", st1), ("kTs2", kTs2), ("v_aug", v_aug), ("qT_s", qT_s),
                        ("bias_col", bias_col), ("attnT", attnT), ("att_o", att_o),
                        ("xt", xt), ("sm", sm), ("sc2col", sc2col), ("sc11", sc11),
                        ("h2T", h2T), ("ffT", ffT), ("P_sb", P_sb), ("h1t_s", h1t_s),
                        ("nacc", nacc), ("accA", accA), ("rrow", rrow), ("dbg1", dbg1),
                        ("arena", arena_t), ("dbg2", dbg2)]
            for eng in ("sync", "gpsimd"):
                W(eng, ("out", ctr["out"]))
                W(eng, ("dve", ctr["dve"]))
                W(eng, ("act", ctr["act"]))
                W(eng, ("pe", ctr["pe"]))
                W(eng, ("sc", ctr["sc"]))
            for nm, tens in dbg_list:
                dd = nc.dram_tensor("dbg_" + nm, list(tens.shape), F32,
                                    kind="ExternalOutput").ap()
                if tens.dtype == F32:
                    EI("sync", lambda e, dd=dd, tens=tens: e.dma_start(out=dd, in_=tens[:, :]), "out", 16)
                else:
                    EI("gpsimd", lambda e, dd=dd, tens=tens: e.dma_start(out=dd, in_=tens[:, :]), "out", 16)
        W("sync", ("out", ctr["out"]))

        with nc.Block() as blk:
            @blk.tensor
            def _(e):
                for fn in ops["tensor"]:
                    fn(e)

            @blk.scalar
            def _(e):
                for fn in ops["scalar"]:
                    fn(e)

            @blk.vector
            def _(e):
                for fn in ops["vector"]:
                    fn(e)

            @blk.gpsimd
            def _(e):
                for fn in ops["gpsimd"]:
                    fn(e)

            @blk.sync
            def _(e):
                for fn in ops["sync"]:
                    fn(e)
    return nc


def kernel(L0_attn_x, L0_attn_w_q, L0_attn_w_k, L0_attn_w_v, L0_attn_w_out,
           L0_attn_softmax_scale, L0_ffn_W_r, L0_ffn_W_k, L0_ffn_W_v,
           L0_ffn_mu_r, L0_ffn_mu_k):
    x = np.ascontiguousarray(np.asarray(L0_attn_x, dtype=np.float32))
    common = {
        "wq": np.ascontiguousarray(np.asarray(L0_attn_w_q, np.float32)),
        "wk": np.ascontiguousarray(np.asarray(L0_attn_w_k, np.float32)),
        "wv": np.ascontiguousarray(np.asarray(L0_attn_w_v, np.float32)),
        "wo": np.ascontiguousarray(np.asarray(L0_attn_w_out, np.float32)),
        "scl": np.ascontiguousarray(np.asarray(L0_attn_softmax_scale, np.float32)),
        "fr": np.ascontiguousarray(np.asarray(L0_ffn_W_r, np.float32)),
        "fk": np.ascontiguousarray(np.asarray(L0_ffn_W_k, np.float32)),
        "fv": np.ascontiguousarray(np.asarray(L0_ffn_W_v, np.float32)),
        "mur": np.ascontiguousarray(np.asarray(L0_ffn_mu_r, np.float32)),
        "muk": np.ascontiguousarray(np.asarray(L0_ffn_mu_k, np.float32)),
    }
    nc = build_core()
    in_maps = [dict(common, x=np.ascontiguousarray(x[b])) for b in range(8)]
    res = run_bass_kernel_spmd(nc, in_maps, core_ids=list(range(8)))
    return np.stack([res.results[b]["y"] for b in range(8)], axis=0)


# version 18 (final)
# speedup vs baseline: 1.0069x; 1.0069x over previous
import numpy as np
import contextlib
import concourse.bass as bass
import concourse.mybir as mybir
from concourse.bass_utils import run_bass_kernel_spmd

F32 = mybir.dt.float32
BF16 = mybir.dt.bfloat16
AF = mybir.ActivationFunctionType
ALU = mybir.AluOpType
AX = mybir.AxisListType

D = 1024
S = 4096
NBLK = 16
EPS = 1e-5


SCHED_LOG = []


def build_core(n_sblk=NBLK, dbg=False):
    nc = bass.Bass("TRN2", target_bir_lowering=False)
    x_d = nc.dram_tensor("x", [S, D], F32, kind="ExternalInput").ap()
    wq_d = nc.dram_tensor("wq", [D, D], F32, kind="ExternalInput").ap()
    wk_d = nc.dram_tensor("wk", [64, D], F32, kind="ExternalInput").ap()
    wv_d = nc.dram_tensor("wv", [64, D], F32, kind="ExternalInput").ap()
    wo_d = nc.dram_tensor("wo", [D, D], F32, kind="ExternalInput").ap()
    scl_d = nc.dram_tensor("scl", [1], F32, kind="ExternalInput").ap()
    fr_d = nc.dram_tensor("fr", [D, D], F32, kind="ExternalInput").ap()
    fk_d = nc.dram_tensor("fk", [D, D], F32, kind="ExternalInput").ap()
    fv_d = nc.dram_tensor("fv", [D, D], F32, kind="ExternalInput").ap()
    mur_d = nc.dram_tensor("mur", [D], F32, kind="ExternalInput").ap()
    muk_d = nc.dram_tensor("muk", [D], F32, kind="ExternalInput").ap()
    y_d = nc.dram_tensor("y", [S, D], F32, kind="ExternalOutput").ap()

    x_t = x_d.rearrange("(a p) c -> p a c", p=128)
    y_t = y_d.rearrange("(a p) c -> p a c", p=128)

    ops = {"tensor": [], "scalar": [], "vector": [], "gpsimd": [], "sync": []}
    ctr = {"pe": 0, "act": 0, "dve": 0, "gp": 0, "x": 0, "w": 0, "out": 0, "sc": 0}
    sems = {}
    hwm = {}

    def EI(eng, fn, sem, n=1):
        ctr[sem] += n
        SCHED_LOG.append(("I", eng, sem, ctr[sem], fn.__code__.co_firstlineno))
        ops[eng].append(lambda e, fn=fn, sem=sem, n=n: fn(e).then_inc(sems[sem], n))
        return (sem, ctr[sem])

    def W(eng, tok):
        if tok is None:
            return
        sem, v = tok
        if v <= 0 or hwm.get((eng, sem), -1) >= v:
            return
        hwm[(eng, sem)] = v
        ops[eng].append(lambda e, sem=sem, v=v: e.wait_ge(sems[sem], v))

    def tmax(*toks):
        toks = [t for t in toks if t is not None and t[1] > 0]
        if not toks:
            return None
        dom = toks[0][0]
        assert all(t[0] == dom for t in toks), toks
        return (dom, max(t[1] for t in toks))

    _st = contextlib.ExitStack()
    with _st:
        def sb(name, shape, dt=BF16):
            return _st.enter_context(nc.sbuf_tensor(name, shape, dt))
        wq_t = sb("wq_t", [128, 8192])
        wo_t = sb("wo_t", [128, 8192])
        fr_t = sb("fr_t", [128, 8192])
        fk_t = sb("fk_t", [128, 8192])
        fv_t = sb("fv_t", [128, 8192])
        wkv_t = sb("wkv_t", [128, 1024])
        arena_t = sb("arena", [128, 8192])
        wnat = arena_t[:, :]
        xrT = arena_t[:, 0:2048]
        xkT = arena_t[:, 2048:4096]
        rT = arena_t[:, 4096:6144]
        kaT = arena_t[:, 6144:8192]
        kTs2 = sb("kTs2", [128, 4096])
        v_aug = sb("v_aug", [128, 2080])
        xt = sb("xt", [128, 6144], F32)
        st1 = sb("st1", [128, 64], F32)
        h1tok = sb("h1tok", [128, 1024])
        h1t_s = sb("h1t_s", [128, 2048])
        qT_s = sb("qT_s", [128, 4096])
        qsq = sb("qsq", [128, 512])
        nacc = sb("nacc", [1, 512], F32)
        P_sb = sb("P_sb", [128, 2048])
        attnT = sb("attnT", [128, 4096])
        accA = sb("accA", [128, 512])
        rrow = sb("rrow", [128, 512])
        recipb = sb("recipb", [128, 512])
        normB = sb("normB", [128, 256])
        att_o = sb("att_o", [128, 2048], F32)
        h2tok = sb("h2tok", [128, 1024])
        h2tok2 = sb("h2tok2", [128, 1024])
        h2T = sb("h2T", [128, 2056])
        diffT = sb("diffT", [128, 2048])
        ffT = sb("ffT", [128, 2048])
        katmp = sb("katmp", [128, 256])
        ksq = sb("ksq", [128, 256])
        vT_tmp = sb("vT_tmp", [128, 256])
        out_sb = sb("out_sb", [128, 2048], F32)
        stt = sb("stt", [128, 32], F32)
        sacc = sb("sacc", [128, 8], F32)
        junk = sb("junk", [128, 1024])
        dbg1 = sb("dbg1", [128, 16], F32)
        dbg2 = sb("dbg2", [128, 1024], F32)
        ident = sb("ident", [128, 128])
        ones2 = sb("ones2", [128, 128])
        onesc = sb("onesc", [128, 1])
        mur_c = sb("mur_c", [128, 8], F32)
        muk_c = sb("muk_c", [128, 8], F32)
        epsc = sb("epsc", [128, 1], F32)
        cinv = sb("cinv", [128, 1], F32)
        czero = sb("czero", [128, 1], F32)
        cmhalf = sb("cmhalf", [128, 1], F32)
        sc11 = sb("sc11", [1, 2], F32)
        smb = sb("smb", [1, 4])
        sm = sb("sm", [1, 16], F32)
        bias_col = sb("bias_col", [128, 2], F32)
        sc2col = sb("sc2col", [128, 1], F32)
        ring = _st.enter_context(nc.psum_tensor("ring", [128, 2048], F32))
        acc2 = _st.enter_context(nc.psum_tensor("acc2", [128, 1024], F32))
        util_t = _st.enter_context(nc.psum_tensor("util", [128, 2048], BF16))
        for nm in ["pe", "act", "dve", "gp", "x", "w", "out", "sc"]:
            sems[nm] = _st.enter_context(nc.semaphore("s_" + nm))

        ub = [util_t[:, 0:1024], util_t[:, 1024:2048]]
        uf = [util_t[:, 0:1024].bitcast(F32), util_t[:, 1024:2048].bitcast(F32)]
        ub_cv = [u.rearrange("p (c j) -> p c j", c=8) for u in ub]
        P3 = P_sb[:, :].rearrange("p (b n) -> p b n", n=512)
        h1t_v = h1t_s[:, :].rearrange("p (c v j) -> p c v j", c=8, v=2)
        h2T_v = h2T[:, :].rearrange("p (c w) -> p c w", w=257)

        h1tok_b = [h1tok, h2tok2]

        util_free = [None, None]
        uidx = [0]

        import os as _os
        _single_util = _os.environ.get("KBISECT_UTIL1") == "1"
        def uacq():
            if _single_util:
                return 0
            k = uidx[0]
            uidx[0] ^= 1
            return k

        def dspace(k=1):
            for _ in range(k):
                EI("vector", lambda e: e.tensor_copy(dbg1[:, 0:1], czero[:, 0:1]), "dve")

        def pe_margin(sem="pe"):
            EI("tensor", lambda e: e.ldweights(ident[:, 0:128]), sem)
            return EI("tensor", lambda e: e.ldweights(ident[:, 0:128]), sem)

        def gp_const(g):
            g.memset(ident[:], 0.0)
            g.affine_select(out=ident[:], in_=ident[:], compare_op=ALU.not_equal,
                            fill=1.0, base=0, pattern=[[-1, 128]], channel_multiplier=1)
            g.memset(ones2[:], 0.0)
            g.memset(ones2[0:1, :], 1.0)
            g.memset(ones2[64:65, :], 1.0)
            g.memset(onesc[:], 1.0)
            g.memset(epsc[:], EPS)
            g.memset(cinv[:], 1.0 / D)
            g.memset(czero[:], 0.0)
            g.memset(cmhalf[:], -0.5)
            g.memset(sm[:], 0.0)
            g.memset(v_aug[:], 1.0)
            return g.memset(h2T[:], 0.0)
        gp_done = EI("gpsimd", gp_const, "gp")

        for j in range(8):
            EI("gpsimd", lambda g, j=j: g.dma_start(
                out=mur_c[:, j:j + 1],
                in_=mur_d[128 * j:128 * j + 128].rearrange("(p o) -> p o", o=1)), "w", 16)
            EI("gpsimd", lambda g, j=j: g.dma_start(
                out=muk_c[:, j:j + 1],
                in_=muk_d[128 * j:128 * j + 128].rearrange("(p o) -> p o", o=1)), "w", 16)
        w_small = EI("gpsimd", lambda g: g.dma_start(
            out=sc11[0:1, 0:1], in_=scl_d.rearrange("(p o) -> p o", o=1)), "w", 16)

        W("scalar", gp_done)
        W("vector", gp_done)
        W("vector", w_small)
        W("scalar", w_small)
        a_ln = EI("scalar", lambda e: e.activation(sc11[0:1, 1:2], sc11[0:1, 0:1], AF.Ln,
                                                   bias=0.0, scale=1.0), "act")
        W("vector", a_ln)
        d_ng = EI("vector", lambda e: e.tensor_scalar_mul(sc11[0:1, 1:2], sc11[0:1, 1:2],
                                                          cmhalf[0:1, :]), "dve")
        W("scalar", d_ng)
        a_sc2 = EI("scalar", lambda e: e.activation(sc11[0:1, 1:2], sc11[0:1, 1:2], AF.Exp,
                                                    bias=0.0, scale=-1.0), "act")
        W("vector", a_sc2)
        d_sc2 = EI("vector", lambda e: e.tensor_copy(smb[0:1, 0:1], sc11[0:1, 1:2]), "dve")
        W("tensor", d_sc2)
        W("tensor", gp_done)
        k0 = uacq()
        p_sc2 = EI("tensor", lambda e, k0=k0: e.matmul(uf[k0][:, 0:1], ones2[0:1, :],
                                                       smb[0:1, 0:1], start=True, stop=True), "pe")
        p_sc2 = pe_margin()
        W("vector", p_sc2)
        util_free[k0] = EI("vector", lambda e, k0=k0: e.tensor_copy(sc2col[:], uf[k0][:, 0:1]), "dve")

        def stage_weight_thunks(loads, n_ci, emit_tr, emit_ev):
            th = []
            wtok = {}
            def t_dma():
                W("gpsimd", ("pe", ctr["pe"]))
                for ld in loads:
                    wtok["w"] = EI("gpsimd", ld, "w", 16)
            th.append(t_dma)
            def mk(ci):
                def f():
                    k = uacq()
                    W("tensor", wtok["w"])
                    W("tensor", util_free[k])
                    pc = EI("tensor", lambda e, ci=ci, k=k: emit_tr(e, ci, k), "pe")
                    pc = pe_margin()
                    W("vector", pc)
                    util_free[k] = EI("vector", lambda e, ci=ci, k=k: emit_ev(e, ci, k), "dve")
                return f
            for ci in range(n_ci):
                th.append(mk(ci))
            return th

        def big_tr(wt):
            def tr(e, ci, k):
                for oj in range(8):
                    ins = e.transpose(ub[k][:, 128 * oj:128 * oj + 128],
                                      wnat[:, 1024 * oj + 128 * ci:1024 * oj + 128 * ci + 128],
                                      ident[:])
                return ins
            def ev(e, ci, k):
                return e.tensor_copy(wt[:, 1024 * ci:1024 * ci + 1024], ub[k][:, :])
            return tr, ev

        stage_pending = []
        for wd, wt in [(wq_d, wq_t), (wo_d, wo_t), (fr_d, fr_t), (fk_d, fk_t), (fv_d, fv_t)]:
            tr, ev = big_tr(wt)
            stage_pending += stage_weight_thunks([lambda g, wd=wd: g.dma_start(
                out=wnat.rearrange("p (a c) -> p a c", c=1024),
                in_=wd.rearrange("(a p) c -> p a c", p=128))], 8, tr, ev)

        def kv_tr(e, ci, k):
            return e.transpose(ub[k][:, 0:128], wnat[:, 128 * ci:128 * ci + 128], ident[:])
        def kv_ev(e, ci, k):
            return e.tensor_copy(wkv_t[:, 128 * ci:128 * ci + 128], ub[k][:, 0:128])
        for f in stage_weight_thunks([lambda g: g.dma_start(out=arena_t[0:64, 0:1024], in_=wv_d),
                                      lambda g: g.dma_start(out=arena_t[64:128, 0:1024], in_=wk_d)],
                                     8, kv_tr, kv_ev):
            f()

        def emit_ln(src_ap_fn, ready_dve, ready_act, o=0):
            c = 8 * o
            W("vector", ready_dve)
            EI("vector", lambda e, c=c: e.reduce_sum(stt[:, c:c + 1], src_ap_fn(), axis=AX.X), "dve")
            dspace()
            dm = EI("vector", lambda e, c=c: e.tensor_scalar_mul(stt[:, c + 5:c + 6],
                                                                 stt[:, c:c + 1], cinv[:]), "dve")
            dspace()
            W("scalar", ready_act)
            asq = EI("scalar", lambda e, o=o: e.activation(junk[:, :], src_ap_fn(), AF.Square,
                                                           bias=0.0, scale=1.0,
                                                           accum_out=sacc[:, o:o + 1]), "act")
            W("vector", asq)
            EI("vector", lambda e, c=c: e.tensor_mul(stt[:, c + 2:c + 3], stt[:, c + 5:c + 6],
                                                     stt[:, c + 5:c + 6]), "dve")
            dspace()
            EI("vector", lambda e, c=c, o=o: e.tensor_scalar(stt[:, c + 3:c + 4], sacc[:, o:o + 1],
                                                             cinv[:], stt[:, c + 2:c + 3],
                                                             ALU.mult, ALU.subtract), "dve")
            dspace()
            dvar = ("dve", ctr["dve"])
            W("scalar", dvar)
            aln = EI("scalar", lambda e, c=c: e.activation(stt[:, c + 4:c + 5], stt[:, c + 3:c + 4],
                                                           AF.Ln, bias=epsc[:], scale=1.0), "act")
            W("vector", aln)
            EI("vector", lambda e, c=c: e.tensor_scalar_mul(stt[:, c + 7:c + 8], stt[:, c + 4:c + 5],
                                                            cmhalf[:]), "dve")
            dspace()
            dng = ("dve", ctr["dve"])
            W("scalar", dng)
            arstd = EI("scalar", lambda e, c=c: e.activation(stt[:, c + 6:c + 7], stt[:, c + 7:c + 8],
                                                             AF.Exp, bias=0.0, scale=1.0), "act")
            return dm, arstd

        h1t_sb = [h1t_s, P_sb]
        h1t_vb = [h1t_s[:, :].rearrange("p (c v j) -> p c v j", c=8, v=2),
                  P_sb[:, :].rearrange("p (c v j) -> p c v j", c=8, v=2)]
        ln1_dve = [None] * NBLK
        ln1_act = [None] * NBLK
        kmax_done = None

        def pa_head(b):
            slot = b % 3
            if b >= 3:
                W("sync", ln1_dve[b - 3])
                W("sync", ln1_act[b - 3])
            xcnt = EI("sync", lambda e, b=b, slot=slot: e.dma_start(
                out=xt[:, 2048 * slot:2048 * slot + 2048].rearrange("p (a c) -> p a c", c=1024),
                in_=x_t[:, 2 * b:2 * b + 2, :]), "x", 16)
            for u in range(2):
                t = 2 * b + u
                xcol = 2048 * slot + 1024 * u
                c = 8 * u
                dm, arstd = emit_ln(lambda xcol=xcol: xt[:, xcol:xcol + 1024], xcnt, xcnt, o=u)
                ln1_act[b] = arstd
                W("vector", arstd)
                dspace()
                EI("vector", lambda e, t=t, c=c: e.tensor_copy(st1[:, 2 * t:2 * t + 1],
                                                               stt[:, c + 5:c + 6]), "dve")
                EI("vector", lambda e, t=t, c=c: e.tensor_copy(st1[:, 2 * t + 1:2 * t + 2],
                                                               stt[:, c + 6:c + 7]), "dve")
                EI("vector", lambda e, xcol=xcol, u=u, c=c: e.tensor_scalar(
                    h1tok_b[u][:, :], xt[:, xcol:xcol + 1024], stt[:, c + 5:c + 6],
                    stt[:, c + 6:c + 7], ALU.subtract, ALU.mult), "dve")
                dspace()
                dh1 = ("dve", ctr["dve"])
                ln1_dve[b] = dh1
                k = uacq()
                W("tensor", dh1)
                W("tensor", util_free[k])
                def trh(e, k=k, u=u):
                    for ci in range(8):
                        ins = e.transpose(ub[k][:, 128 * ci:128 * ci + 128],
                                          h1tok_b[u][:, 128 * ci:128 * ci + 128], ident[:])
                    return ins
                pc = EI("tensor", trh, "pe")
                pc = pe_margin()
                W("vector", pc)
                util_free[k] = EI("vector", lambda e, u=u, k=k, b=b: e.tensor_copy(
                    h1t_vb[b % 2][:, :, u, :], ub_cv[k]), "dve")

        def pa_tail(b):
            nonlocal kmax_done
            W("tensor", util_free[uidx[0] ^ 1])
            W("tensor", util_free[uidx[0]])
            def kvmm(e, b=b):
                for ci in range(8):
                    e.matmul(ring[:, 0:256], wkv_t[:, 128 * ci:128 * ci + 128],
                             h1t_sb[b % 2][:, 256 * ci:256 * ci + 256], start=(ci == 0),
                             stop=(ci == 7), skip_group_check=True)
                for ci in range(8):
                    ins = e.matmul(ring[0:64, 512:768], wkv_t[:, 128 * ci + 64:128 * ci + 128],
                                   h1t_sb[b % 2][:, 256 * ci:256 * ci + 256], start=(ci == 0),
                                   stop=(ci == 7), skip_group_check=True)
                return ins
            pkv = EI("tensor", kvmm, "pe")
            pkv = pe_margin()
            W("vector", pkv)
            EI("vector", lambda e, b=b: e.tensor_scalar_mul(
                kTs2[64:128, 256 * b:256 * b + 256], ring[64:128, 0:256], sc2col[64:128, :]), "dve")
            EI("vector", lambda e, b=b: e.tensor_scalar_mul(
                kTs2[0:64, 256 * b:256 * b + 256], ring[0:64, 512:768], sc2col[0:64, :]), "dve")
            EI("vector", lambda e: e.tensor_copy(vT_tmp[0:64, 0:256], ring[0:64, 0:256]), "dve")
            dspace()
            dkv = EI("vector", lambda e, b=b: e.tensor_mul(
                ksq[64:128, 0:256], kTs2[64:128, 256 * b:256 * b + 256],
                kTs2[64:128, 256 * b:256 * b + 256]), "dve")
            k = uacq()
            W("tensor", dkv)
            W("tensor", util_free[k])
            def vtr(e, k=k):
                for u2 in range(2):
                    e.transpose(ub[k][:, 64 * u2:64 * u2 + 64],
                                vT_tmp[0:64, 128 * u2:128 * u2 + 128], ident[0:64, 0:64])
                return e.matmul(uf[k][0:1, 256:512], onesc[64:128, 0:1], ksq[64:128, 0:256],
                                start=True, stop=True, skip_group_check=True)
            pvt = EI("tensor", vtr, "pe")
            pvt = pe_margin()
            W("vector", pvt)
            EI("vector", lambda e, b=b, k=k: e.tensor_copy(
                v_aug[:, :].rearrange("p (t j) -> p t j", j=65)[:, 2 * b:2 * b + 2, 0:64],
                ub[k][:, 0:128].rearrange("p (t j) -> p t j", j=64)), "dve")
            EI("vector", lambda e, k=k: e.reduce_max(sm[0:1, 1:2], uf[k][0:1, 256:512], axis=AX.X), "dve")
            dspace()
            util_free[k] = EI("vector", lambda e: e.tensor_max(sm[0:1, 0:1], sm[0:1, 0:1],
                                                               sm[0:1, 1:2]), "dve")
            kmax_done = util_free[k]

        for b in range(NBLK):
            if b >= 1:
                pa_tail(b - 1)
            pa_head(b)
            for _ in range(3):
                if stage_pending:
                    stage_pending.pop(0)()
        pa_tail(NBLK - 1)
        while stage_pending:
            stage_pending.pop(0)()

        bank_free_act = [None] * 4
        fin_tok = [None, None]
        fin_dve = None
        x_slot_dve = {13 % 3: ln1_dve[13], 14 % 3: ln1_dve[14], 15 % 3: ln1_dve[15]}
        x_slot_act = {13 % 3: ln1_act[13], 14 % 3: ln1_act[14], 15 % 3: ln1_act[15]}
        xb_cnt = [None] * (NBLK + 2)
        out_cnt = [None] * (NBLK + 2)
        bias_done = [None] * (NBLK + 1)

        NG = 16

        def emit_xdma(i):
            if i >= n_sblk:
                return
            slot = i % 3
            W("sync", x_slot_dve.get(slot))
            W("sync", x_slot_act.get(slot))
            xb_cnt[i] = EI("sync", lambda e, i=i, slot=slot: e.dma_start(
                out=xt[:, 2048 * slot:2048 * slot + 2048].rearrange("p (a c) -> p a c", c=1024),
                in_=x_t[:, 2 * i:2 * i + 2, :]), "x", 16)

        def b0_thunks(i):
            if i >= n_sblk:
                return []
            th = []
            par = i % 2
            xbase = 2048 * (i % 3)
            qb = 2048 * par
            cell = {}

            def t_h1a(u):
                def f():
                    t = 2 * i + u
                    xcol = xbase + 1024 * u
                    W("vector", xb_cnt[i])
                    if u == 1:
                        W("vector", cell.get("trh1_0"))
                    EI("vector", lambda e, xcol=xcol, t=t: e.tensor_scalar(
                        h1tok[:, :], xt[:, xcol:xcol + 1024], st1[:, 2 * t:2 * t + 1],
                        st1[:, 2 * t + 1:2 * t + 2], ALU.subtract, ALU.mult), "dve")
                    dspace()
                    cell["dh1"] = ("dve", ctr["dve"])
                return f

            def t_h1b(u):
                def f():
                    k = uacq()
                    W("tensor", cell["dh1"])
                    W("tensor", util_free[k])
                    def trh(e, k=k):
                        for ci in range(8):
                            ins = e.transpose(ub[k][:, 128 * ci:128 * ci + 128],
                                              h1tok[:, 128 * ci:128 * ci + 128], ident[:])
                        return ins
                    pc = EI("tensor", trh, "pe")
                    pc = pe_margin()
                    cell["trh1_%d" % u] = pc
                    W("vector", pc)
                    util_free[k] = EI("vector", lambda e, u=u, k=k: e.tensor_copy(
                        h1t_v[:, :, u, :], ub_cv[k]), "dve")
                    dspace()
                    cell["h1ev%d" % u] = ("dve", ctr["dve"])
                return f

            def t_q(j):
                def f():
                    k = uacq()
                    W("tensor", util_free[k])
                    if j == 0:
                        W("tensor", tmax(cell["h1ev0"], cell["h1ev1"]))
                    def qmm(e, j=j, k=k):
                        for ci in range(8):
                            ins = e.matmul(uf[k][:, 0:256],
                                           wq_t[:, 1024 * ci + 128 * j:1024 * ci + 128 * j + 128],
                                           h1t_s[:, 256 * ci:256 * ci + 256],
                                           start=(ci == 0), stop=(ci == 7), skip_group_check=True)
                        if j >= 2:
                            ins = e.matmul(uf[k][0:1, 256:512], onesc[:, 0:1],
                                           qsq[:, 256 * (j % 2):256 * (j % 2) + 256],
                                           start=False, stop=True, skip_group_check=True)
                        return ins
                    pq = EI("tensor", qmm, "pe")
                    pq = pe_margin()
                    W("vector", pq)
                    EI("vector", lambda e, j=j, k=k: e.tensor_scalar_mul(
                        qT_s[:, qb + 256 * j:qb + 256 * j + 256], uf[k][:, 0:256], sc2col[:]), "dve")
                    dspace()
                    EI("vector", lambda e, j=j: e.tensor_mul(
                        qsq[:, 256 * (j % 2):256 * (j % 2) + 256],
                        qT_s[:, qb + 256 * j:qb + 256 * j + 256],
                        qT_s[:, qb + 256 * j:qb + 256 * j + 256]), "dve")
                    if j == 2:
                        EI("vector", lambda e, k=k: e.tensor_copy(
                            nacc[0:1, 256 * par:256 * par + 256], uf[k][0:1, 256:512]), "dve")
                    elif j > 2:
                        EI("vector", lambda e, k=k: e.tensor_max(
                            nacc[0:1, 256 * par:256 * par + 256],
                            nacc[0:1, 256 * par:256 * par + 256], uf[k][0:1, 256:512]), "dve")
                    util_free[k] = ("dve", ctr["dve"])
                return f

            def t_bound_a():
                def f():
                    toks = []
                    for jj in (6, 7):
                        k = uacq()
                        W("tensor", util_free[k])
                        EI("tensor", lambda e, jj=jj, k=k: e.matmul(
                            uf[k][0:1, 256:512], onesc[:, 0:1],
                            qsq[:, 256 * (jj % 2):256 * (jj % 2) + 256],
                            start=True, stop=True, skip_group_check=True), "pe")
                        toks.append((k, pe_margin()))
                    for k, tok in toks:
                        W("vector", tok)
                        util_free[k] = EI("vector", lambda e, k=k: e.tensor_max(
                            nacc[0:1, 256 * par:256 * par + 256],
                            nacc[0:1, 256 * par:256 * par + 256], uf[k][0:1, 256:512]), "dve")
                    dspace()
                    EI("vector", lambda e: e.reduce_max(sm[0:1, 2 + 4 * par:3 + 4 * par],
                                                        nacc[0:1, 256 * par:256 * par + 256],
                                                        axis=AX.X), "dve")
                    dspace()
                    W("vector", kmax_done)
                    EI("vector", lambda e: e.tensor_add(sm[0:1, 3 + 4 * par:4 + 4 * par],
                                                        sm[0:1, 2 + 4 * par:3 + 4 * par],
                                                        sm[0:1, 0:1]), "dve")
                    dspace()
                    EI("vector", lambda e: e.tensor_scalar_mul(
                        sm[0:1, 4 + 4 * par:5 + 4 * par], sm[0:1, 3 + 4 * par:4 + 4 * par],
                        cmhalf[0:1, :]), "dve")
                    W("vector", ("act", ctr["act"]))
                    dspace()
                    cell["dbias"] = EI("vector", lambda e: e.tensor_copy(
                        smb[0:1, 1 + par:2 + par], sm[0:1, 4 + 4 * par:5 + 4 * par]), "dve")
                return f

            def t_bound_b():
                def f():
                    k = uacq()
                    W("tensor", cell["dbias"])
                    W("tensor", util_free[k])
                    EI("tensor", lambda e, k=k: e.matmul(uf[k][:, 0:1], ones2[0:1, :],
                                                         smb[0:1, 1 + par:2 + par],
                                                         start=True, stop=True,
                                                         skip_group_check=True), "pe")
                    pb = pe_margin()
                    W("vector", pb)
                    util_free[k] = EI("vector", lambda e, k=k: e.tensor_copy(
                        bias_col[:, par:par + 1], uf[k][:, 0:1]), "dve")
                    dspace()
                    bias_done[i] = ("dve", ctr["dve"])
                return f

            th.append(t_h1a(0))
            th.append("gap")
            th.append(t_h1b(0))
            th.append(t_h1a(1))
            th.append("gap")
            th.append(t_h1b(1))
            for j in range(8):
                th.append(t_q(j))
            th.append(t_bound_a())
            th.append("gap")
            th.append(t_bound_b())
            return th

        h2tok_b = [h2tok, h2tok2]

        def b2_thunks(i):
            if i < 0 or i >= n_sblk:
                return []
            th = []
            par = i % 2
            xbase = 2048 * (i % 3)
            ab = 2048 * par
            cell = {}

            def t_wo(s_sub, half):
                def f():
                    nonlocal fin_dve
                    k = uacq()
                    W("vector", out_cnt[i])
                    W("tensor", util_free[k])
                    W("tensor", fin_tok[0])
                    def womm_a(e, s_sub=s_sub, half=half, k=k):
                        for ci in range(7):
                            ins = e.matmul(
                                uf[k][:, 0:512],
                                attnT[:, ab + 256 * ci + 128 * s_sub:ab + 256 * ci + 128 * s_sub + 128],
                                wo_t[:, 1024 * ci + 512 * half:1024 * ci + 512 * half + 512],
                                start=(ci == 0), stop=False, skip_group_check=True)
                        return ins
                    EI("tensor", womm_a, "pe")
                    W("tensor", fin_dve)
                    def womm_b(e, s_sub=s_sub, half=half, k=k):
                        return e.matmul(
                            uf[k][:, 0:512],
                            attnT[:, ab + 256 * 7 + 128 * s_sub:ab + 256 * 7 + 128 * s_sub + 128],
                            wo_t[:, 1024 * 7 + 512 * half:1024 * 7 + 512 * half + 512],
                            start=False, stop=True, skip_group_check=True)
                    pw = EI("tensor", womm_b, "pe")
                    pw = pe_margin()
                    W("vector", pw)
                    util_free[k] = EI("vector", lambda e, s_sub=s_sub, half=half, k=k: e.tensor_copy(
                        att_o[:, 1024 * s_sub + 512 * half:1024 * s_sub + 512 * half + 512],
                        uf[k][:, 0:512]), "dve")
                    cell["ev_%d_%d" % (s_sub, half)] = util_free[k]
                return f

            def t_ln_a(s_sub):
                def f():
                    xcol = xbase + 1024 * s_sub
                    ac0 = 1024 * s_sub
                    c = 8 * s_sub
                    c2 = 16 + 8 * s_sub
                    o2 = 4 + s_sub
                    EI("vector", lambda e, xcol=xcol, c2=c2: e.reduce_sum(
                        stt[:, c2:c2 + 1], xt[:, xcol:xcol + 1024], axis=AX.X), "dve")
                    dspace()
                    EI("vector", lambda e, c2=c2: e.tensor_scalar_mul(
                        stt[:, c2 + 5:c2 + 6], stt[:, c2:c2 + 1], cinv[:]), "dve")
                    dspace()
                    EI("vector", lambda e, c2=c2: e.tensor_mul(
                        stt[:, c2 + 2:c2 + 3], stt[:, c2 + 5:c2 + 6], stt[:, c2 + 5:c2 + 6]), "dve")
                    dspace()
                    dm, arstd = emit_ln(lambda ac0=ac0: att_o[:, ac0:ac0 + 1024],
                                        None,
                                        tmax(cell["ev_%d_0" % s_sub], cell["ev_%d_1" % s_sub]),
                                        o=s_sub)
                    W("vector", arstd)
                    dspace()
                    EI("vector", lambda e, ac0=ac0, s_sub=s_sub, c=c: e.tensor_scalar(
                        out_sb[:, 1024 * s_sub:1024 * s_sub + 1024], att_o[:, ac0:ac0 + 1024],
                        stt[:, c + 5:c + 6], stt[:, c + 6:c + 7], ALU.subtract, ALU.mult), "dve")
                    dspace()
                    EI("vector", lambda e, xcol=xcol, s_sub=s_sub: e.tensor_add(
                        xt[:, xcol:xcol + 1024], xt[:, xcol:xcol + 1024],
                        out_sb[:, 1024 * s_sub:1024 * s_sub + 1024]), "dve")
                    dspace()
                    dadd2 = ("dve", ctr["dve"])
                    W("scalar", dadd2)
                    asq2 = EI("scalar", lambda e, xcol=xcol, o2=o2: e.activation(
                        junk[:, :], xt[:, xcol:xcol + 1024], AF.Square, bias=0.0, scale=1.0,
                        accum_out=sacc[:, o2:o2 + 1]), "act")
                    W("vector", asq2)
                    EI("vector", lambda e, c2=c2, o2=o2: e.tensor_scalar(
                        stt[:, c2 + 3:c2 + 4], sacc[:, o2:o2 + 1], cinv[:],
                        stt[:, c2 + 2:c2 + 3], ALU.mult, ALU.subtract), "dve")
                    dspace()
                    dvar2 = ("dve", ctr["dve"])
                    W("scalar", dvar2)
                    aln2 = EI("scalar", lambda e, c2=c2: e.activation(
                        stt[:, c2 + 4:c2 + 5], stt[:, c2 + 3:c2 + 4], AF.Ln,
                        bias=epsc[:], scale=1.0), "act")
                    W("vector", aln2)
                    EI("vector", lambda e, c2=c2: e.tensor_scalar_mul(
                        stt[:, c2 + 7:c2 + 8], stt[:, c2 + 4:c2 + 5], cmhalf[:]), "dve")
                    dspace()
                    dng2 = ("dve", ctr["dve"])
                    W("scalar", dng2)
                    arstd2 = EI("scalar", lambda e, c2=c2: e.activation(
                        stt[:, c2 + 6:c2 + 7], stt[:, c2 + 7:c2 + 8], AF.Exp,
                        bias=0.0, scale=1.0), "act")
                    W("vector", arstd2)
                    dspace()
                    EI("vector", lambda e, xcol=xcol, s_sub=s_sub, c2=c2: e.tensor_scalar(
                        h2tok_b[s_sub][:, :],
                        xt[:, xcol:xcol + 1024], stt[:, c2 + 5:c2 + 6], stt[:, c2 + 6:c2 + 7],
                        ALU.subtract, ALU.mult), "dve")
                    dspace()
                    cell["dh2_%d" % s_sub] = ("dve", ctr["dve"])
                return f

            def t_ln_b(s_sub):
                def f():
                    k = uacq()
                    W("tensor", cell["dh2_%d" % s_sub])
                    W("tensor", util_free[k])
                    def trh2(e, k=k, s_sub=s_sub):
                        for ci in range(8):
                            ins = e.transpose(ub[k][:, 128 * ci:128 * ci + 128],
                                              h2tok_b[s_sub][:, 128 * ci:128 * ci + 128], ident[:])
                        return ins
                    pc = EI("tensor", trh2, "pe")
                    pc = pe_margin()
                    cell["trh2_%d" % s_sub] = pc
                    W("vector", pc)
                    util_free[k] = EI("vector", lambda e, s_sub=s_sub, k=k: e.tensor_copy(
                        h2T_v[:, :, 1 + 128 * s_sub:1 + 128 * s_sub + 128], ub_cv[k]), "dve")
                return f

            def t_mix():
                def f():
                    for ci in range(8):
                        c0 = 257 * ci
                        EI("vector", lambda e, c0=c0, ci=ci: e.tensor_sub(
                            diffT[:, 256 * ci:256 * ci + 256], h2T[:, c0 + 1:c0 + 257],
                            h2T[:, c0:c0 + 256]), "dve")
                    for ci in range(8):
                        c0 = 257 * ci
                        EI("vector", lambda e, c0=c0, ci=ci: e.scalar_tensor_tensor(
                            xkT[:, 256 * ci:256 * ci + 256], diffT[:, 256 * ci:256 * ci + 256],
                            muk_c[:, ci:ci + 1], h2T[:, c0:c0 + 256], ALU.mult, ALU.add), "dve")
                    cell["mix"] = ("dve", ctr["dve"])
                    for ci in range(8):
                        c0 = 257 * ci
                        EI("vector", lambda e, c0=c0, ci=ci: e.scalar_tensor_tensor(
                            xrT[:, 256 * ci:256 * ci + 256], diffT[:, 256 * ci:256 * ci + 256],
                            mur_c[:, ci:ci + 1], h2T[:, c0:c0 + 256], ALU.mult, ALU.add), "dve")
                    for ci in range(8):
                        c0 = 257 * ci
                        EI("vector", lambda e, c0=c0: e.tensor_copy(
                            h2T[:, c0:c0 + 1], h2T[:, c0 + 256:c0 + 257]), "dve")
                return f

            def t_fk(j):
                def f():
                    k = uacq()
                    W("tensor", util_free[k])
                    W("tensor", cell["mix"])
                    def fkmm(e, j=j, k=k):
                        for ci in range(8):
                            ins = e.matmul(uf[k][:, 0:256],
                                           fk_t[:, 1024 * ci + 128 * j:1024 * ci + 128 * j + 128],
                                           xkT[:, 256 * ci:256 * ci + 256],
                                           start=(ci == 0), stop=(ci == 7), skip_group_check=True)
                        return ins
                    pk = EI("tensor", fkmm, "pe")
                    pk = pe_margin()
                    W("vector", pk)
                    EI("vector", lambda e, k=k: e.tensor_scalar_max(katmp[:, 0:256],
                                                                    uf[k][:, 0:256], czero[:]), "dve")
                    dspace()
                    util_free[k] = EI("vector", lambda e, j=j: e.tensor_mul(
                        kaT[:, 256 * j:256 * j + 256], katmp[:, 0:256], katmp[:, 0:256]), "dve")
                return f

            def t_fr(j):
                def f():
                    k = uacq()
                    W("tensor", util_free[k])
                    def frmm(e, j=j, k=k):
                        for ci in range(8):
                            ins = e.matmul(uf[k][:, 0:256],
                                           fr_t[:, 1024 * ci + 128 * j:1024 * ci + 128 * j + 128],
                                           xrT[:, 256 * ci:256 * ci + 256],
                                           start=(ci == 0), stop=(ci == 7), skip_group_check=True)
                        return ins
                    pr = EI("tensor", frmm, "pe")
                    pr = pe_margin()
                    W("vector", pr)
                    util_free[k] = EI("vector", lambda e, j=j, k=k: e.tensor_copy(
                        diffT[:, 256 * j:256 * j + 256], uf[k][:, 0:256]), "dve")
                    if j == 7:
                        W("scalar", util_free[k])
                        cell["sig"] = EI("scalar", lambda e: e.activation(
                            rT[:, 0:2048], diffT[:, 0:2048], AF.Sigmoid,
                            bias=0.0, scale=1.0), "act")
                return f

            def t_fv(j):
                def f():
                    k = uacq()
                    W("tensor", util_free[k])
                    if j == 0:
                        W("tensor", cell["sig"])
                        pe_margin()
                    def fvmm(e, j=j, k=k):
                        for ci in range(8):
                            ins = e.matmul(uf[k][:, 0:256],
                                           fv_t[:, 1024 * ci + 128 * j:1024 * ci + 128 * j + 128],
                                           kaT[:, 256 * ci:256 * ci + 256],
                                           start=(ci == 0), stop=(ci == 7), skip_group_check=True)
                        return ins
                    pv = EI("tensor", fvmm, "pe")
                    pv = pe_margin()
                    W("vector", pv)
                    util_free[k] = EI("vector", lambda e, j=j, k=k: e.tensor_mul(
                        ffT[:, 256 * j:256 * j + 256], uf[k][:, 0:256],
                        rT[:, 256 * j:256 * j + 256]), "dve")
                    if j == 7:
                        cell["ff"] = util_free[k]
                return f

            def t_out_a(s_sub):
                def f():
                    k = uacq()
                    W("tensor", util_free[k])
                    W("tensor", cell["ff"])
                    def trf(e, s_sub=s_sub, k=k):
                        for ci in range(8):
                            ins = e.transpose(ub[k][:, 128 * ci:128 * ci + 128],
                                              ffT[:, 256 * ci + 128 * s_sub:256 * ci + 128 * s_sub + 128],
                                              ident[:])
                        return ins
                    pf = EI("tensor", trf, "pe")
                    cell["pf_%d" % s_sub] = pe_margin()
                    cell["uk_%d" % s_sub] = k
                return f

            def t_out_b(s_sub):
                def f():
                    k = cell["uk_%d" % s_sub]
                    pf = cell["pf_%d" % s_sub]
                    xcol = xbase + 1024 * s_sub
                    c = 8 * s_sub
                    o = 2 + s_sub
                    W("vector", pf)
                    EI("vector", lambda e, k=k, c=c: e.reduce_sum(stt[:, c:c + 1], ub[k][:, :],
                                                                  axis=AX.X), "dve")
                    dspace()
                    EI("vector", lambda e, c=c: e.tensor_scalar_mul(stt[:, c + 5:c + 6],
                                                                    stt[:, c:c + 1], cinv[:]), "dve")
                    dspace()
                    dm3 = ("dve", ctr["dve"])
                    W("scalar", dm3)
                    asq = EI("scalar", lambda e, k=k, o=o: e.activation(junk[:, :], ub[k][:, :],
                                                                        AF.Square, bias=0.0, scale=1.0,
                                                                        accum_out=sacc[:, o:o + 1]), "act")
                    W("vector", asq)
                    EI("vector", lambda e, c=c: e.tensor_mul(stt[:, c + 2:c + 3], stt[:, c + 5:c + 6],
                                                             stt[:, c + 5:c + 6]), "dve")
                    dspace()
                    EI("vector", lambda e, c=c, o=o: e.tensor_scalar(stt[:, c + 3:c + 4],
                                                                     sacc[:, o:o + 1], cinv[:],
                                                                     stt[:, c + 2:c + 3], ALU.mult,
                                                                     ALU.subtract), "dve")
                    dspace()
                    dvar = ("dve", ctr["dve"])
                    W("scalar", dvar)
                    aln = EI("scalar", lambda e, c=c: e.activation(stt[:, c + 4:c + 5],
                                                                   stt[:, c + 3:c + 4], AF.Ln,
                                                                   bias=epsc[:], scale=1.0), "act")
                    W("vector", aln)
                    EI("vector", lambda e, c=c: e.tensor_scalar_mul(stt[:, c + 7:c + 8],
                                                                    stt[:, c + 4:c + 5],
                                                                    cmhalf[:]), "dve")
                    dspace()
                    dng = ("dve", ctr["dve"])
                    W("scalar", dng)
                    arstd = EI("scalar", lambda e, c=c: e.activation(stt[:, c + 6:c + 7],
                                                                     stt[:, c + 7:c + 8], AF.Exp,
                                                                     bias=0.0, scale=1.0), "act")
                    W("vector", arstd)
                    dspace()
                    util_free[k] = EI("vector", lambda e, s_sub=s_sub, k=k, c=c: e.tensor_scalar(
                        out_sb[:, 1024 * s_sub:1024 * s_sub + 1024], ub[k][:, :],
                        stt[:, c + 5:c + 6], stt[:, c + 6:c + 7], ALU.subtract, ALU.mult), "dve")
                    dspace()
                    dadd = EI("vector", lambda e, xcol=xcol, s_sub=s_sub: e.tensor_add(
                        out_sb[:, 1024 * s_sub:1024 * s_sub + 1024],
                        out_sb[:, 1024 * s_sub:1024 * s_sub + 1024],
                        xt[:, xcol:xcol + 1024]), "dve")
                    if s_sub == 1:
                        x_slot_dve[i % 3] = dadd
                        x_slot_act[i % 3] = ("act", ctr["act"])
                        W("sync", dadd)
                        out_cnt[i + 1] = EI("sync", lambda e, i=i: e.dma_start(
                            out=y_t[:, 2 * i:2 * i + 2, :],
                            in_=out_sb[:, :].rearrange("p (a c) -> p a c", c=1024)), "out", 16)
                return f

            th.append("gap")
            th.append("gap")
            for s_sub in range(2):
                for half in range(2):
                    th.append(t_wo(s_sub, half))
            th.append(t_ln_a(0))
            th.append(t_ln_a(1))
            th.append("gap")
            th.append("gap")
            th.append("gap")
            th.append(t_ln_b(0))
            th.append("gap")
            th.append(t_ln_b(1))
            th.append(t_mix())
            th.append("gap")
            for j in range(8):
                th.append(t_fk(j))
            for j in range(8):
                th.append(t_fr(j))
            for j in range(8):
                th.append(t_fv(j))
            th.append(t_out_a(0))
            th.append(t_out_a(1))
            th.append("gap")
            th.append(t_out_b(0))
            th.append("gap")
            th.append(t_out_b(1))
            th.append("gap")
            th.append("gap")
            return th

        fin_pending = []

        def spine(i, pending):
            nonlocal fin_dve
            par = i % 2
            qb = 2048 * par
            ab = 2048 * par
            n_slots = 8 * NG
            n0 = len(pending)
            done = [0]
            slot = [0]

            def inject():
                slot[0] += 1
                if fin_pending:
                    fe = fin_pending.pop(0)
                    if fe != "gap":
                        fe()
                target = (n0 * slot[0]) // n_slots
                while pending and done[0] < target:
                    pe_ = pending.pop(0)
                    if pe_ != "gap":
                        pe_()
                    done[0] += 1

            NHPS = 8
            TOT = NHPS * NG

            def hp_of(G):
                return G // NG

            def emit_scores(G):
                g = G % NG
                hp = hp_of(G)
                fb = 2 * (G % 2)
                t0 = 2 * g
                W("tensor", tmax(bank_free_act[fb], bank_free_act[fb + 1]))
                def sc(e, t0=t0, fb=fb, hp=hp):
                    for u in range(2):
                        t = t0 + u
                        off = 256 * u
                        e.matmul(ring[:, 512 * fb + off:512 * fb + off + 256],
                                 kTs2[0:64, 128 * t:128 * t + 128],
                                 qT_s[0:64, qb + 256 * hp:qb + 256 * hp + 256],
                                 start=True, stop=True, tile_position=(0, 0),
                                 skip_group_check=True)
                        ins = e.matmul(ring[:, 512 * (fb + 1) + off:512 * (fb + 1) + off + 256],
                                       kTs2[64:128, 128 * t:128 * t + 128],
                                       qT_s[64:128, qb + 256 * hp:qb + 256 * hp + 256],
                                       start=True, stop=True, tile_position=(64, 0),
                                       skip_group_check=True)
                    return ins
                sc_tok[G] = EI("tensor", sc, "sc")
                return sc_tok[G]

            exp_tok = {}
            sc_tok = {}
            av_tok = {}

            def emit_exp(G):
                fb = 2 * (G % 2)
                W("scalar", ("sc", ctr["sc"]))
                W("scalar", bias_done[i])
                ac = EI("scalar", lambda e, fb=fb: e.activation(
                    P_sb[:, 512 * fb:512 * fb + 1024],
                    ring[:, 512 * fb:512 * fb + 1024],
                    AF.Exp, bias=bias_col[:, par:par + 1], scale=1.0), "act")
                exp_tok[G] = ac
                bank_free_act[fb] = ac
                bank_free_act[fb + 1] = ac

            def emit_attnv(G):
                g = G % NG
                hp = hp_of(G)
                apar = hp % 2
                accp = acc2[:, 512 * apar:512 * apar + 512]
                fb = 2 * (G % 2)
                t0 = 2 * g
                W("tensor", exp_tok.pop(G))
                if t0 == 0:
                    W("tensor", fin_tok[apar])
                EI("tensor", lambda e: e.ldweights(ident[:, 0:128]), "sc")
                def av(e, t0=t0, fb=fb, accp=accp):
                    for u in range(2):
                        t = t0 + u
                        off = 256 * u
                        ins = e.matmul(accp[0:65, 0:512], v_aug[:, 65 * t:65 * t + 65],
                                       P3[:, fb:fb + 2, off:off + 256],
                                       start=(t == 0), stop=(t == 31),
                                       skip_group_check=True)
                    return ins
                av_tok[G] = EI("tensor", av, "sc")
                return av_tok[G]

            def emit_fin(hp):
                nonlocal fin_dve
                apar = hp % 2
                accp = acc2[:, 512 * apar:512 * apar + 512]
                pav = pe_margin("sc")
                W("vector", pav)
                def _recip(e, accp=accp):
                    with nc.allow_low_precision(reason="per-head softmax scale; LN-invariant"):
                        return e.reciprocal(rrow[64:65, 0:512], accp[64:65, 0:512])
                EI("vector", _recip, "dve")
                EI("vector", lambda e, accp=accp: e.tensor_copy(
                    accA[0:64, 0:512], accp[0:64, 0:512]), "dve")
                if dbg and i == 0 and hp == 0:
                    EI("vector", lambda e, accp=accp: e.tensor_copy(
                        dbg2[0:65, 0:512], accp[0:65, 0:512]), "dve")
                dspace()
                fd1 = ("dve", ctr["dve"])

                def F_b(fd1=fd1, accp=accp):
                    W("tensor", fd1)
                    EI("tensor", lambda e: e.matmul(accp[0:64, 0:512], ones2[64:65, 0:64],
                                                    rrow[64:65, 0:512], start=True, stop=True,
                                                    skip_group_check=True), "sc")
                    fin_pending_tok["fp1"] = pe_margin("sc")

                def F_c(hp=hp, accp=accp):
                    W("vector", fin_pending_tok["fp1"])
                    EI("vector", lambda e, accp=accp: e.tensor_copy(
                        recipb[0:64, 0:512], accp[0:64, 0:512]), "dve")
                    dspace()
                    EI("vector", lambda e, hp=hp: e.tensor_mul(
                        attnT[0:64, ab + 256 * hp:ab + 256 * hp + 256], accA[0:64, 0:256],
                        recipb[0:64, 0:256]), "dve")
                    EI("vector", lambda e: e.tensor_mul(
                        normB[0:64, 0:256], accA[0:64, 256:512], recipb[0:64, 256:512]), "dve")
                    dspace()
                    fin_pending_tok["fd2"] = ("dve", ctr["dve"])

                def F_d(accp=accp):
                    W("tensor", fin_pending_tok["fd2"])
                    EI("tensor", lambda e, accp=accp: e.matmul(
                        accp[64:128, 0:256], ident[0:64, 0:64], normB[0:64, 0:256],
                        start=True, stop=True, tile_position=(0, 64),
                        skip_group_check=True), "sc")
                    fin_pending_tok["fp2"] = pe_margin("sc")

                def F_e(hp=hp, apar=apar, accp=accp, i=i):
                    nonlocal fin_dve
                    W("vector", fin_pending_tok["fp2"])
                    EI("vector", lambda e, hp=hp, accp=accp: e.tensor_copy(
                        attnT[64:128, ab + 256 * hp:ab + 256 * hp + 256],
                        accp[64:128, 0:256]), "dve")
                    dspace()
                    tok = ("dve", ctr["dve"])
                    if i == 0 and hp == 0 and dbg:
                        tok = EI("vector", lambda e: e.tensor_copy(
                            dbg2[:, 512:768], attnT[0:128, 0:256]), "dve")
                    fin_tok[apar] = tok
                    fin_dve = tok

                fin_pending.extend(["gap", "gap", "gap", "gap", F_b, F_c, "gap", F_d, "gap", F_e])

            for G in range(TOT):
                if G >= 1:
                    emit_exp(G - 1)
                emit_scores(G)
                if G >= 2:
                    emit_attnv(G - 2)
                    if (G - 2) % NG == NG - 1:
                        emit_fin(hp_of(G - 2))
                inject()
            emit_exp(TOT - 1)
            emit_attnv(TOT - 2)
            emit_attnv(TOT - 1)
            emit_fin(NHPS - 1)
            while fin_pending:
                fe = fin_pending.pop(0)
                if fe != "gap":
                    fe()
            while pending:
                pe_ = pending.pop(0)
                if pe_ != "gap":
                    pe_()

        fin_pending_tok = {}

        emit_xdma(0)
        for f in b0_thunks(0):
            if f != "gap":
                f()
        emit_xdma(1)
        for i in range(n_sblk):
            pending = b2_thunks(i - 1) + b0_thunks(i + 1)
            spine(i, pending)
            emit_xdma(i + 2)
        for f in b2_thunks(n_sblk - 1):
            if f != "gap":
                f()

        if dbg:
            dbg_list = [("st1", st1), ("kTs2", kTs2), ("v_aug", v_aug), ("qT_s", qT_s),
                        ("bias_col", bias_col), ("attnT", attnT), ("att_o", att_o),
                        ("xt", xt), ("sm", sm), ("sc2col", sc2col), ("sc11", sc11),
                        ("h2T", h2T), ("ffT", ffT), ("P_sb", P_sb), ("h1t_s", h1t_s),
                        ("nacc", nacc), ("accA", accA), ("rrow", rrow), ("dbg1", dbg1),
                        ("arena", arena_t), ("dbg2", dbg2)]
            for eng in ("sync", "gpsimd"):
                W(eng, ("out", ctr["out"]))
                W(eng, ("dve", ctr["dve"]))
                W(eng, ("act", ctr["act"]))
                W(eng, ("pe", ctr["pe"]))
                W(eng, ("sc", ctr["sc"]))
            for nm, tens in dbg_list:
                dd = nc.dram_tensor("dbg_" + nm, list(tens.shape), F32,
                                    kind="ExternalOutput").ap()
                if tens.dtype == F32:
                    EI("sync", lambda e, dd=dd, tens=tens: e.dma_start(out=dd, in_=tens[:, :]), "out", 16)
                else:
                    EI("gpsimd", lambda e, dd=dd, tens=tens: e.dma_start(out=dd, in_=tens[:, :]), "out", 16)
        W("sync", ("out", ctr["out"]))

        with nc.Block() as blk:
            @blk.tensor
            def _(e):
                for fn in ops["tensor"]:
                    fn(e)

            @blk.scalar
            def _(e):
                for fn in ops["scalar"]:
                    fn(e)

            @blk.vector
            def _(e):
                for fn in ops["vector"]:
                    fn(e)

            @blk.gpsimd
            def _(e):
                for fn in ops["gpsimd"]:
                    fn(e)

            @blk.sync
            def _(e):
                for fn in ops["sync"]:
                    fn(e)
    return nc


def kernel(L0_attn_x, L0_attn_w_q, L0_attn_w_k, L0_attn_w_v, L0_attn_w_out,
           L0_attn_softmax_scale, L0_ffn_W_r, L0_ffn_W_k, L0_ffn_W_v,
           L0_ffn_mu_r, L0_ffn_mu_k):
    x = np.ascontiguousarray(np.asarray(L0_attn_x, dtype=np.float32))
    common = {
        "wq": np.ascontiguousarray(np.asarray(L0_attn_w_q, np.float32)),
        "wk": np.ascontiguousarray(np.asarray(L0_attn_w_k, np.float32)),
        "wv": np.ascontiguousarray(np.asarray(L0_attn_w_v, np.float32)),
        "wo": np.ascontiguousarray(np.asarray(L0_attn_w_out, np.float32)),
        "scl": np.ascontiguousarray(np.asarray(L0_attn_softmax_scale, np.float32)),
        "fr": np.ascontiguousarray(np.asarray(L0_ffn_W_r, np.float32)),
        "fk": np.ascontiguousarray(np.asarray(L0_ffn_W_k, np.float32)),
        "fv": np.ascontiguousarray(np.asarray(L0_ffn_W_v, np.float32)),
        "mur": np.ascontiguousarray(np.asarray(L0_ffn_mu_r, np.float32)),
        "muk": np.ascontiguousarray(np.asarray(L0_ffn_mu_k, np.float32)),
    }
    nc = build_core()
    in_maps = [dict(common, x=np.ascontiguousarray(x[b])) for b in range(8)]
    res = run_bass_kernel_spmd(nc, in_maps, core_ids=list(range(8)))
    return np.stack([res.results[b]["y"] for b in range(8)], axis=0)
